# version 11
# baseline (speedup 1.0000x reference)
import math
from contextlib import ExitStack

import numpy as np
import ml_dtypes

import concourse.bass as bass
import concourse.mybir as mybir
from concourse.bass_utils import run_bass_kernel_spmd

F32 = mybir.dt.float32
BF16 = mybir.dt.bfloat16
I32 = mybir.dt.int32
I8 = mybir.dt.int8
AF = mybir.ActivationFunctionType
ALU = mybir.AluOpType

B, S, D = 4, 4096, 4096
H, KV, HD = 32, 8, 128
II = 11008
NT = 32
NISL = II // 128
NCC = 16
EPS = 1e-5
ISQ = 1.0 / math.sqrt(HD)

EVEN_CHUNKS = sorted([4 * k for k in range(4)] + [4 * k + 3 for k in range(4)])
ODD_CHUNKS = sorted([4 * k + 1 for k in range(4)] + [4 * k + 2 for k in range(4)])
CHUNKS = EVEN_CHUNKS

ENGINES = ("sync", "gpsimd", "scalar", "vector", "tensor")


class Prog:

    def __init__(self, sems):
        self.sems = sems
        self.count = {k: 0 for k in ENGINES}
        self.steps = {k: [] for k in ENGINES}

    def emit(self, eng, fn, after=(), signal=False, tag=""):
        inc = 16 if eng in ("sync", "gpsimd") else 1
        wd = {}
        for ev in after:
            if ev is None:
                continue
            e2, v = ev
            if e2 == eng:
                continue
            assert v <= self.count[e2], f"future wait [{tag}]: {e2}@{v} > {self.count[e2]}"
            wd[e2] = max(wd.get(e2, 0), v)
        self.steps[eng].append((wd, fn, inc if signal else 0))
        if signal:
            self.count[eng] += inc
            return (eng, self.count[eng])
        return None

    def ev(self, eng):
        return (eng, self.count[eng])

    def barrier(self):
        evs = [self.ev(e) for e in ENGINES]
        for e in ENGINES:
            self.emit(e, None, after=evs, tag="barrier")

    def build(self, block):
        def mk(eng):
            def body(e):
                for wd, fn, inc in self.steps[eng]:
                    for e2, v in wd.items():
                        e.wait_ge(self.sems[e2], v)
                    if fn is not None:
                        inst = fn(e)
                        if inc:
                            inst.then_inc(self.sems[eng], inc)
            return body
        block.sync(mk("sync"))
        block.gpsimd(mk("gpsimd"))
        block.scalar(mk("scalar"))
        block.vector(mk("vector"))
        block.tensor(mk("tensor"))


_LN_STATE = {"var": None, "bc": None, "sum_mm": None}


def ln_stats(P, T, acc_s, acc_q, ps_st, ps_bc, n_feat, bc_inv, bc_nm, after_acc,
             direct=False):
    st_m, st_msq, st_var, st_std, st_inv, st_nm = T["stat6"]
    pv = _LN_STATE["var"]
    pb = _LN_STATE["bc"]
    e1 = P.emit("tensor", lambda e: e.matmul(
        ps_st[0:1, 0:256], T["ones_col"][:, 0:1], acc_s, start=True, stop=True),
        after=list(after_acc) + [pv], signal=True, tag="ln_sum")
    e2 = P.emit("tensor", lambda e: e.matmul(
        ps_st[0:1, 256:512], T["ones_col"][:, 0:1], acc_q, start=True, stop=True),
        after=list(after_acc), signal=True, tag="ln_sumsq")
    _LN_STATE["sum_mm"] = e2
    inv_n = 1.0 / n_feat
    P.emit("vector", lambda e: e.tensor_scalar_mul(st_m[:], ps_st[0:1, 0:256], float(inv_n)),
           after=[e1, e2], tag="ln_m")
    P.emit("vector", lambda e: e.tensor_tensor(
        st_msq[:], st_m[:], st_m[:], ALU.mult), tag="ln_msq")
    ev = P.emit("vector", lambda e: e.scalar_tensor_tensor(
        st_var[:], ps_st[0:1, 256:512], float(inv_n), st_msq[:], ALU.mult, ALU.subtract),
        after=[e2], signal=True, tag="ln_var")
    _LN_STATE["var"] = ev
    es = P.emit("scalar", lambda e: e.activation(
        st_std[:], st_var[:], AF.Sqrt, bias=np.float32(EPS)), after=[ev], signal=True, tag="ln_std")
    P.emit("vector", lambda e: e.reciprocal(st_inv[:], st_std[:]),
           after=[es], tag="ln_inv")
    ev2 = P.emit("vector", lambda e: e.tensor_tensor(
        st_nm[:], st_m[:], st_inv[:], ALU.mult), signal=True, tag="ln_nm")
    eb1 = P.emit("tensor", lambda e: e.matmul(
        ps_bc[:, 0:256], T["ones_row"][0:1, :], st_inv[:], start=True, stop=True),
        after=[ev2, pb], signal=True, tag="ln_bci")
    eb2 = P.emit("tensor", lambda e: e.matmul(
        ps_bc[:, 256:512], T["ones_row"][0:1, :], st_nm[:], start=True, stop=True),
        after=[ev2], signal=True, tag="ln_bcn")
    if direct:
        return eb1, eb2
    ec1 = P.emit("scalar", lambda e: e.copy(bc_inv, ps_bc[:, 0:256]),
                 after=[eb1, eb2] + list(after_acc), signal=True, tag="ln_cpi")
    ec2 = P.emit("scalar", lambda e: e.copy(bc_nm, ps_bc[:, 256:512]),
                 after=[eb2] + list(after_acc), signal=True, tag="ln_cpn")
    _LN_STATE["bc"] = ec2
    return ec1, ec2


PHASE_LIMIT = None


def build_nc(taps=()):
    _LN_STATE.update({"var": None, "bc": None, "sum_mm": None})
    nc = bass.Bass("TRN2", target_bir_lowering=False)

    def dram(name, shape, dtype, kind="ExternalInput"):
        return nc.dram_tensor(name, shape, dtype, kind=kind).ap()

    def scratch(name, shape, dtype):
        kind = "ExternalOutput" if name in taps else "Internal"
        return nc.dram_tensor(name, shape, dtype, kind=kind).ap()

    x_cm = dram("x_cm", [NCC, 128, NT, 256], F32)
    wk_cm = dram("wk_cm", [128, NT, 1024], F32)
    wv_cm = dram("wv_cm", [128, NT, 1024], F32)
    wq_cm = dram("wq_cm", [NT, 128, NT, 128], F32)
    wo_cm = dram("wo_cm", [NT, 128, NT, 128], F32)
    qwg_cm = dram("qwg_cm", [NISL, 128, D], I8)
    qwu_cm = dram("qwu_cm", [NISL, 128, D], I8)
    qwd_cm = dram("qwd_cm", [NT, 128, II], I8)
    gu_sc = dram("gu_sc", [NISL, 128, 4, 32], F32)
    d_sc = dram("d_sc", [NT, 128, 2, NISL], F32)
    rope_q = dram("rope_q", [2, 64, 2048], F32)
    rope_k = dram("rope_k", [2, 64, S], F32)
    masks_in = dram("masks_in", [8, 128, 1024], BF16)
    ident_in = dram("ident_in", [128, 128], BF16)
    out_cm = dram("out_cm", [4, 128, NT, 512], F32, kind="ExternalOutput")

    xln_sc = scratch("xln_sc", [NCC, 128, NT, 256], BF16)
    q_sc = scratch("q_sc", [8, 128, NT, 256], BF16)
    k_sc = scratch("k_sc", [128, KV, S], BF16)
    v_sc = scratch("v_sc", [128, NT, 1024], BF16)
    ao_sc = scratch("ao_sc", [8, 128, NT, 256], BF16)
    y_sc = scratch("y_sc", [4, 128, NT, 512], F32)
    z_sc = scratch("z_sc", [4, 128, NT, 512], F32)
    x2_sc = scratch("x2_sc", [8, 128, NT, 256], BF16)
    h_sc = scratch("h_sc", [4, 128, NISL, 512], BF16)
    wd_sc = scratch("wd_sc", [NT, 128, NISL, 128], BF16)
    o2_sc = scratch("o2_sc", [4, 128, NT, 512], F32)

    with ExitStack() as top:
        sems = {k: top.enter_context(nc.semaphore(f"sem_{k}")) for k in ENGINES}
        P = Prog(sems)

        def sb(st, name, shape, dt):
            return st.enter_context(nc.sbuf_tensor(name, shape, dt))

        def pt(st, name, shape, dt=F32):
            return st.enter_context(nc.psum_tensor(name, shape, dt))

        _stop = {}

        def run_phase(nm, fn):
            if _stop.get("s"):
                return
            fn()
            if PHASE_LIMIT is not None and nm == PHASE_LIMIT:
                _stop["s"] = True

        T = {}
        T["ones_col"] = sb(top, "ones_col", [128, 1], F32)
        T["ones_row"] = sb(top, "ones_row", [1, 128], F32)
        T["ident"] = sb(top, "ident", [128, 128], BF16)
        T["stat6"] = [sb(top, f"stat{i}", [1, 256], F32) for i in range(6)]

        eid = P.emit("sync", lambda e: e.dma_start(out=T["ident"][:], in_=ident_in),
                     signal=True, tag="ld_ident")
        P.emit("vector", lambda e: e.memset(T["ones_col"][:], 1.0), signal=True)
        P.emit("vector", lambda e: e.memset(T["ones_row"][:], 1.0), signal=True)

        def _phase_a():
         with ExitStack() as ph:
            xb = sb(ph, "a_x", [128, NT, 256], F32)
            lnb = [sb(ph, "a_l0", [128, NT, 256], BF16), sb(ph, "a_l1", [128, NT, 256], BF16)]
            wkb = sb(ph, "a_wk", [128, NT, 1024], BF16)
            wvb = sb(ph, "a_wv", [128, NT, 1024], BF16)
            acc_s = sb(ph, "a_as", [128, 256], F32)
            acc_q = sb(ph, "a_aq", [128, 256], F32)
            tmp = sb(ph, "a_tmp", [128, 256], F32)
            kout0 = sb(ph, "a_k0", [128, 256], BF16)
            kout = [kout0, kout0]
            vout0 = sb(ph, "a_v0", [128, 1024], BF16)
            vout = [vout0, vout0]
            cosk0 = sb(ph, "a_c0", [64, 256], F32)
            sink0 = sb(ph, "a_s0", [64, 256], F32)
            cosk = [cosk0, cosk0]
            sink = [sink0, sink0]
            t64a = tmp[0:64, :]
            t64b = sb(ph, "a_t64b", [64, 256], F32)[:]
            ps_st = pt(ph, "a_st", [128, 512])
            ps_bc = pt(ph, "a_bc", [128, 512])
            ps_k = [pt(ph, "a_pk0", [128, 512]), pt(ph, "a_pk1", [128, 512])]
            ps_v = [pt(ph, "a_pv0", [128, 512]), pt(ph, "a_pv1", [128, 512])]

            ewk = P.emit("gpsimd", lambda e: e.dma_start(out=wkb[:], in_=wk_cm),
                         signal=True, tag="a_wk")
            ewv = P.emit("gpsimd", lambda e: e.dma_start(out=wvb[:], in_=wv_cm),
                         signal=True, tag="a_wv")

            aev = None
            rope_used = [None, None]
            kv_mm_done = [None, None]
            pe_free = {0: None, 1: None}
            st_ev = {0: None, 1: None}
            ev_free = {0: None, 1: None}
            st2_ev = {0: None, 1: None}
            for cc in range(NCC):
                par = cc % 2
                lb = lnb[par]
                ck = cosk[par]
                sk = sink[par]
                rc_ev = P.emit("sync", lambda e, cc=cc, b=ck: e.dma_start(
                    out=b[:], in_=rope_k[0, :, 256 * cc:256 * (cc + 1)]),
                    after=[rope_used[1 - par]], signal=True, tag=f"a_rc{cc}")
                rs_ev = P.emit("sync", lambda e, cc=cc, b=sk: e.dma_start(
                    out=b[:], in_=rope_k[1, :, 256 * cc:256 * (cc + 1)]),
                    signal=True, tag=f"a_rs{cc}")
                x_ev = P.emit("sync", lambda e, cc=cc: e.dma_start(
                    out=xb[:], in_=x_cm[cc]), after=[aev], signal=True, tag=f"a_x{cc}")
                for t in range(NT):
                    if t == 0:
                        P.emit("vector", lambda e: e.tensor_copy(acc_s[:], xb[:, 0, :]),
                               after=[x_ev, _LN_STATE["sum_mm"]], tag="a_c")
                        P.emit("vector", lambda e: e.tensor_tensor(
                            acc_q[:], xb[:, 0, :], xb[:, 0, :], ALU.mult), tag="a_q0")
                    else:
                        P.emit("vector", lambda e, t=t: e.tensor_tensor(
                            acc_s[:], acc_s[:], xb[:, t, :], ALU.add), tag="a_s")
                        P.emit("vector", lambda e, t=t: e.tensor_tensor(
                            tmp[:], xb[:, t, :], xb[:, t, :], ALU.mult), tag="a_m")
                        last = P.emit("vector", lambda e: e.tensor_tensor(
                            acc_q[:], acc_q[:], tmp[:], ALU.add),
                            signal=(t == NT - 1), tag="a_q")
                e_inv, e_nm = ln_stats(P, T, acc_s[:], acc_q[:], ps_st, ps_bc, D,
                                       None, None, [last], direct=True)
                for t in range(NT):
                    P.emit("vector", lambda e, t=t: e.tensor_tensor(
                        tmp[:], xb[:, t, :], ps_bc[:, 0:256], ALU.mult),
                        after=[e_inv, e_nm] if t == 0 else (), tag="a_p1")
                    aev = P.emit("vector", lambda e, o=lb, t=t: e.tensor_tensor(
                        o[:, t, :], tmp[:], ps_bc[:, 256:512], ALU.subtract),
                        after=([kv_mm_done[par]] if t == 0 else ()),
                        signal=(t == NT - 1), tag="a_p2")
                _LN_STATE["bc"] = aev
                if cc in CHUNKS:
                    P.emit("gpsimd", lambda e, cc=cc, b=lb: e.dma_start(
                        out=xln_sc[cc], in_=b[:]), after=[aev], signal=True,
                        tag=f"a_lnst{cc}")
                for ot in range(KV):
                    pp = ot % 2
                    pk = ps_k[pp]
                    for t in range(NT):
                        evm = P.emit("tensor", lambda e, b=lb, t=t, ot=ot, p=pk: e.matmul(
                            p[:, 0:256], wkb[:, t, 128 * ot:128 * (ot + 1)], b[:, t, :],
                            start=(t == 0), stop=(t == NT - 1)),
                            after=([aev, ewk, pe_free[pp]] if t == 0 else ()),
                            signal=(t == NT - 1), tag=f"a_kmm{cc}_{ot}")
                    ko = kout[pp]
                    P.emit("vector", lambda e, p=pk, ck=ck: e.tensor_tensor(
                        t64a, p[0:64, 0:256], ck[:], ALU.mult),
                        after=[evm, rc_ev, rs_ev, st_ev[0], st_ev[1]], tag="a_r1")
                    P.emit("vector", lambda e, p=pk, sk=sk: e.tensor_tensor(
                        t64b, p[64:128, 0:256], sk[:], ALU.mult), tag="a_r2")
                    P.emit("vector", lambda e, ko=ko: e.tensor_tensor(
                        ko[0:64, :], t64a, t64b, ALU.subtract), tag="a_r3")
                    P.emit("vector", lambda e, p=pk, ck=ck: e.tensor_tensor(
                        t64a, p[64:128, 0:256], ck[:], ALU.mult), tag="a_r4")
                    P.emit("vector", lambda e, p=pk, sk=sk: e.tensor_tensor(
                        t64b, p[0:64, 0:256], sk[:], ALU.mult), tag="a_r5")
                    evr = P.emit("vector", lambda e, ko=ko: e.tensor_tensor(
                        ko[64:128, :], t64a, t64b, ALU.add),
                        signal=True, tag="a_r6")
                    pe_free[pp] = evr
                    st_ev[pp] = P.emit("gpsimd", lambda e, cc=cc, ot=ot, ko=ko: e.dma_start(
                        out=k_sc[:, ot, 256 * cc:256 * (cc + 1)], in_=ko[:]),
                        after=[evr], signal=True, tag=f"a_kst{cc}_{ot}")
                rope_used[par] = P.ev("vector")
                for j in range(2):
                    vo = vout[j]
                    for nh in range(2):
                        pv = ps_v[nh]
                        for t in range(NT):
                            evm = P.emit("tensor", lambda e, b=lb, t=t, j=j, nh=nh, p=pv: e.matmul(
                                p[:], b[:, t, 128 * j:128 * (j + 1)],
                                wvb[:, t, 512 * nh:512 * (nh + 1)],
                                start=(t == 0), stop=(t == NT - 1)),
                                after=([aev, ewv, ev_free[nh]] if t == 0 else ()),
                                signal=(t == NT - 1), tag=f"a_vmm{cc}_{j}_{nh}")
                        ev_free[nh] = P.emit("scalar", lambda e, vo=vo, nh=nh, p=pv: e.copy(
                            vo[:, 512 * nh:512 * (nh + 1)], p[:]),
                            after=[evm, st2_ev[0], st2_ev[1]], signal=True, tag=f"a_vcp{cc}_{j}_{nh}")
                    st2_ev[j] = P.emit("gpsimd", lambda e, cc=cc, j=j, vo=vo: e.dma_start(
                        out=v_sc[:, 2 * cc + j, :], in_=vo[:]),
                        after=[ev_free[0], ev_free[1]], signal=True, tag=f"a_vst{cc}_{j}")
                kv_mm_done[par] = P.ev("tensor")
            P.barrier()

        run_phase("a", _phase_a)

        def _phase_a3():
         with ExitStack() as ph:
            xres = sb(ph, "a3_x", [128, NT, 2048], BF16)
            wqb = [sb(ph, "a3_w0", [128, NT, 128], BF16), sb(ph, "a3_w1", [128, NT, 128], BF16)]
            qout = [sb(ph, "a3_q0", [128, 512], BF16), sb(ph, "a3_q1", [128, 512], BF16)]
            cosq = sb(ph, "a3_cos", [64, 2048], F32)
            sinq = sb(ph, "a3_sin", [64, 2048], F32)
            q64a = sb(ph, "a3_t64a", [64, 512], F32)
            q64b = sb(ph, "a3_t64b", [64, 512], F32)
            ps_q = [pt(ph, "a3_p0", [128, 512]), pt(ph, "a3_p1", [128, 512])]

            P.emit("sync", lambda e: e.dma_start(out=cosq[:], in_=rope_q[0]), signal=True)
            cq = P.emit("sync", lambda e: e.dma_start(out=sinq[:], in_=rope_q[1]), signal=True)
            for s_ in range(8):
                P.emit("sync", lambda e, s_=s_: e.dma_start(
                    out=xres[:, :, 256 * s_:256 * (s_ + 1)], in_=xln_sc[CHUNKS[s_]]),
                    signal=True, tag=f"a3x{s_}")
            xall = P.ev("sync")
            wq_ev = [None, None]
            use_w = [None, None]
            st_ev = {0: None, 1: None}
            pe_free = {0: None, 1: None}
            for ot in range(NT):
                wpar = ot % 2
                wb = wqb[wpar]
                wq_ev[wpar] = P.emit("gpsimd", lambda e, ot=ot, w=wb: e.dma_start(
                    out=w[:], in_=wq_cm[ot]), after=[use_w[wpar]], signal=True, tag=f"a3w{ot}")
                for rg in range(4):
                    pp = rg % 2
                    pq = ps_q[pp]
                    for t in range(NT):
                        evm = P.emit("tensor", lambda e, w=wb, t=t, rg=rg, p=pq: e.matmul(
                            p[:], w[:, t, :], xres[:, t, 512 * rg:512 * (rg + 1)],
                            start=(t == 0), stop=(t == NT - 1)),
                            after=([wq_ev[wpar], xall, pe_free[pp]] if t == 0 else ()),
                            signal=(t == NT - 1), tag=f"a3mm{ot}_{rg}")
                    qo = qout[pp]
                    c0, c1 = 512 * rg, 512 * (rg + 1)
                    P.emit("vector", lambda e, p=pq, c0=c0, c1=c1: e.tensor_tensor(
                        q64a[:], p[0:64, :], cosq[:, c0:c1], ALU.mult),
                        after=[evm, cq, st_ev[pp]], tag="a3r1")
                    P.emit("vector", lambda e, p=pq, c0=c0, c1=c1: e.tensor_tensor(
                        q64b[:], p[64:128, :], sinq[:, c0:c1], ALU.mult), tag="a3r2")
                    P.emit("vector", lambda e, qo=qo: e.tensor_tensor(
                        qo[0:64, :], q64a[:], q64b[:], ALU.subtract), tag="a3r3")
                    P.emit("vector", lambda e, p=pq, c0=c0, c1=c1: e.tensor_tensor(
                        q64a[:], p[64:128, :], cosq[:, c0:c1], ALU.mult), tag="a3r4")
                    P.emit("vector", lambda e, p=pq, c0=c0, c1=c1: e.tensor_tensor(
                        q64b[:], p[0:64, :], sinq[:, c0:c1], ALU.mult), tag="a3r5")
                    evr = P.emit("vector", lambda e, qo=qo: e.tensor_tensor(
                        qo[64:128, :], q64a[:], q64b[:], ALU.add),
                        signal=True, tag="a3r6")
                    pe_free[pp] = evr
                    P.emit("sync", lambda e, ot=ot, rg=rg, qo=qo: e.dma_start(
                        out=q_sc[2 * rg, :, ot, :], in_=qo[:, 0:256]),
                        after=[evr], signal=True, tag=f"a3s{ot}_{rg}a")
                    st_ev[pp] = P.emit("sync", lambda e, ot=ot, rg=rg, qo=qo: e.dma_start(
                        out=q_sc[2 * rg + 1, :, ot, :], in_=qo[:, 256:512]),
                        after=[evr], signal=True, tag=f"a3s{ot}_{rg}b")
                use_w[wpar] = P.ev("tensor")
            P.barrier()

        run_phase("a3", _phase_a3)
        def _phase_b():
         with ExitStack() as ph:
            kbuf = sb(ph, "b_k", [128, KV, S], BF16)
            vbuf = sb(ph, "b_v", [128, NT, 1024], BF16)
            mbuf = [sb(ph, "b_m0", [128, 1024], BF16), sb(ph, "b_m1", [128, 1024], BF16)]
            qsl = [sb(ph, "b_q0", [128, NT, 256], BF16), sb(ph, "b_q1", [128, NT, 256], BF16)]
            psb_ = [sb(ph, "b_p0", [128, 1024], BF16), sb(ph, "b_p1", [128, 1024], BF16),
                    sb(ph, "b_p2", [128, 1024], BF16)]
            lacc4 = sb(ph, "b_l4", [128, 1024], F32)
            t512 = sb(ph, "b_t5", [128, 512], F32)
            lacc = sb(ph, "b_l", [128, 256], F32)
            linv = sb(ph, "b_li", [128, 256], F32)
            linr = sb(ph, "b_lr", [1, 256], F32)
            aosl = sb(ph, "b_a", [128, NT, 256], BF16)
            ps_sc = [pt(ph, "b_s0", [128, 1024]), pt(ph, "b_s1", [128, 1024]),
                     pt(ph, "b_s2", [128, 1024])]
            ps_o = [pt(ph, "b_o0", [128, 512]), pt(ph, "b_o1", [128, 512])]

            ek = P.emit("sync", lambda e: e.dma_start(out=kbuf[:], in_=k_sc), signal=True)
            ev0 = P.emit("sync", lambda e: e.dma_start(out=vbuf[:], in_=v_sc), signal=True)
            base = [ek, ev0]

            NPAR = 3
            st = {
                "q_ev": [None, None], "q_use": [None, None],
                "m_ev": [None, None], "m_use": [None, None],
                "ao_st": None, "gpar": 0,
                "exp_done": {i: None for i in range(NPAR)},
                "l_done": {i: None for i in range(NPAR)},
                "av_done": {i: None for i in range(NPAR)},
                "oev": {0: None, 1: None},
                "prev_el": None, "prev_cl": None, "prev_fold": None,
            }
            pend = [None]

            def flush_pend():
                if pend[0] is None:
                    return
                p_ = pend[0]
                pend[0] = None
                s_, h, g, par, eexp = p_["s"], p_["h"], p_["g"], p_["par"], p_["eexp"]
                kv = h // 4
                hpar = h % 2
                spar = p_["spar"]
                po = ps_o[hpar]
                pb = psb_[par]
                psc = ps_sc[par]
                for j in range(4):
                    kt = 4 * g + j
                    eav = P.emit("tensor", lambda e, kv=kv, kt=kt, j=j, o=pb, po=po, g=g, s_=s_: e.matmul(
                        po[:, 0:256], vbuf[:, kt, 128 * kv:128 * (kv + 1)],
                        o[:, 256 * j:256 * (j + 1)],
                        start=(g == 0 and j == 0), stop=(g == s_ and j == 3)),
                        after=([eexp, st["oev"][hpar]] if (g == 0 and j == 0)
                               else ([eexp] if j == 0 else ())),
                        signal=(g == s_ and j == 3), tag=f"bav{s_}_{h}_{g}")
                st["av_done"][par] = eav
                if g != s_:
                    return
                el = P.emit("tensor", lambda e, psc=psc: e.matmul(
                    psc[0:1, 256:512], T["ones_col"][:, 0:1], lacc[:], start=True, stop=True),
                    after=[p_["fold"]], signal=True, tag=f"blr{s_}_{h}")
                st["prev_el"] = el
                cl = P.emit("scalar", lambda e, psc=psc: e.copy(linr[:], psc[0:1, 256:512]),
                            after=[el, st["prev_cl"]], signal=True, tag="blcl")
                st["prev_cl"] = cl
                eb = P.emit("tensor", lambda e, psc=psc: e.matmul(
                    psc[:, 0:256], T["ones_row"][0:1, :], linr[:], start=True, stop=True),
                    after=[cl], signal=True, tag="blbc")
                rec = P.emit("vector", lambda e, psc=psc: e.reciprocal(linv[:], psc[:, 0:256]),
                             after=[eb], signal=True, tag="brec")
                st["exp_done"][par] = rec
                ao_done = P.emit("vector", lambda e, h=h, po=po: e.tensor_tensor(
                    aosl[:, h, :], po[:, 0:256], linv[:], ALU.mult),
                    after=[eav, rec] + ([st["ao_st"]] if h == 0 else []),
                    signal=True, tag=f"bao{s_}_{h}")
                st["oev"][hpar] = ao_done
                if h == H - 1:
                    st["ao_st"] = P.emit("sync", lambda e, s_=s_: e.dma_start(
                        out=ao_sc[s_], in_=aosl[:]),
                        after=[ao_done], signal=True, tag=f"bst{s_}")

            for s_ in range(8):
                mpar = s_ % 2
                spar = s_ % 2
                q = qsl[spar]
                st["m_ev"][mpar] = P.emit("sync", lambda e, s_=s_, m=mbuf[mpar]: e.dma_start(
                    out=m[:], in_=masks_in[s_]),
                    after=[st["m_use"][mpar]] + (base if s_ == 0 else []),
                    signal=True, tag=f"bm{s_}")
                st["q_ev"][spar] = P.emit("sync", lambda e, s_=s_, q=q: e.dma_start(
                    out=q[:], in_=q_sc[s_]),
                    after=[st["q_use"][spar]] + (base if s_ == 0 else []),
                    signal=True, tag=f"bq{s_}")
                for h in range(H):
                    kv = h // 4
                    leng = "vector" if h % 2 == 0 else "gpsimd"
                    for g in range(s_ + 1):
                        par = st["gpar"] % NPAR
                        st["gpar"] += 1
                        psc = ps_sc[par]
                        pb = psb_[par]
                        for j in range(4):
                            kt = 4 * g + j
                            evm = P.emit("tensor", lambda e, kv=kv, kt=kt, j=j, p=psc, h=h, q=q: e.matmul(
                                p[:, 256 * j:256 * (j + 1)],
                                kbuf[:, kv, 128 * kt:128 * (kt + 1)], q[:, h, :],
                                start=True, stop=True),
                                after=([st["q_ev"][spar], st["exp_done"][par]] if j == 0 else ()),
                                signal=(j == 3), tag=f"bsc{s_}_{h}_{g}")
                        if g == s_:
                            mev2 = P.emit("vector", lambda e, p=psc, m=mbuf[mpar]: e.tensor_tensor(
                                p[:], p[:], m[:], ALU.add),
                                after=[evm, st["m_ev"][mpar]], signal=True, tag=f"bmask{s_}_{h}")
                            eafter = [mev2, st["l_done"][par], st["av_done"][par]]
                        else:
                            eafter = [evm, st["l_done"][par], st["av_done"][par]]
                        eexp = P.emit("scalar", lambda e, p=psc, o=pb: e.activation(
                            o[:], p[:], AF.Exp, scale=np.float32(ISQ)),
                            after=eafter, signal=True, tag=f"bexp{s_}_{h}_{g}")
                        st["exp_done"][par] = eexp
                        if g == 0:
                            lev = P.emit("vector", lambda e, o=pb: e.tensor_copy(lacc4[:], o[:]),
                                         after=[eexp, st["prev_fold"], st["prev_el"]],
                                         signal=True, tag="bl0")
                        else:
                            lev = P.emit(leng, lambda e, o=pb: e.tensor_tensor(
                                lacc4[:], lacc4[:], o[:], ALU.add),
                                after=[eexp] + ([st["l_done"][(par + NPAR - 1) % NPAR]] if g == 1 else []),
                                signal=True, tag="bl")
                        st["l_done"][par] = lev
                        flush_pend()
                        pend[0] = {"s": s_, "h": h, "g": g, "par": par, "eexp": eexp,
                                   "fold": None, "spar": spar}
                        if g == s_:
                            P.emit(leng, lambda e: e.tensor_tensor(
                                t512[:], lacc4[:, 0:512], lacc4[:, 512:1024], ALU.add),
                                after=[lev, st["prev_el"]], tag="bf1")
                            fold = P.emit(leng, lambda e: e.tensor_tensor(
                                lacc[:], t512[:, 0:256], t512[:, 256:512], ALU.add),
                                signal=True, tag="bf2")
                            st["prev_fold"] = fold
                            pend[0]["fold"] = fold
                    if h == H - 1:
                        st["q_use"][spar] = P.ev("tensor")
                        st["m_use"][mpar] = P.ev("vector")
                flush_pend()
            P.barrier()

        run_phase("b", _phase_b)
        with ExitStack() as phc:
            bc2i = sb(phc, "bc2i", [128, 2048], F32)
            bc2n = sb(phc, "bc2n", [128, 2048], F32)
            def _phase_c():
             with ExitStack() as ph:
                aores = sb(ph, "c_ao", [128, NT, 2048], BF16)
                wob = [sb(ph, "c_w0", [128, NT, 128], BF16), sb(ph, "c_w1", [128, NT, 128], BF16)]
                ybuf = [sb(ph, "c_y0", [128, 512], F32), sb(ph, "c_y1", [128, 512], F32)]
                sacc = sb(ph, "c_sa", [128, 2048], F32)
                qacc = sb(ph, "c_qa", [128, 2048], F32)
                sqt = sb(ph, "c_sq", [128, 512], F32)
                ps_y = [pt(ph, "c_p0", [128, 512]), pt(ph, "c_p1", [128, 512])]
                ps_st = pt(ph, "c_st", [128, 512])
                ps_bc = pt(ph, "c_bc", [128, 512])

                for s_ in range(8):
                    P.emit("sync", lambda e, s_=s_: e.dma_start(
                        out=aores[:, :, 256 * s_:256 * (s_ + 1)], in_=ao_sc[s_]),
                        signal=True, tag=f"cao{s_}")
                eao = P.ev("sync")
                w_ev = [None, None]
                w_use = [None, None]
                evac = {0: None, 1: None}
                y_st = {0: None, 1: None}
                stat_last = None
                for ht in range(NT):
                    wpar = ht % 2
                    wb = wob[wpar]
                    w_ev[wpar] = P.emit("gpsimd", lambda e, ht=ht, w=wb: e.dma_start(
                        out=w[:], in_=wo_cm[ht]), after=[w_use[wpar]], signal=True, tag=f"cw{ht}")
                    for rg in range(4):
                        pp = rg % 2
                        py = ps_y[pp]
                        yb = ybuf[pp]
                        for t in range(NT):
                            evm = P.emit("tensor", lambda e, w=wb, t=t, rg=rg, p=py: e.matmul(
                                p[:], w[:, t, :], aores[:, t, 512 * rg:512 * (rg + 1)],
                                start=(t == 0), stop=(t == NT - 1)),
                                after=([w_ev[wpar], eao, evac[pp]] if t == 0 else ()),
                                signal=(t == NT - 1), tag=f"cmm{ht}_{rg}")
                        ec = P.emit("vector", lambda e, yb=yb, p=py: e.tensor_copy(yb[:], p[:]),
                                    after=[evm, y_st[pp]], signal=True, tag=f"ccp{ht}_{rg}")
                        evac[pp] = ec
                        y_st[pp] = P.emit("gpsimd", lambda e, ht=ht, rg=rg, yb=yb: e.dma_start(
                            out=y_sc[rg, :, ht, :], in_=yb[:]), after=[ec], signal=True,
                            tag=f"cst{ht}_{rg}")
                        if ht == 0:
                            P.emit("vector", lambda e, yb=yb, rg=rg: e.tensor_copy(
                                sacc[:, 512 * rg:512 * (rg + 1)], yb[:]), tag="cs0")
                            P.emit("vector", lambda e, yb=yb: e.tensor_tensor(
                                sqt[:], yb[:], yb[:], ALU.mult), tag="cm0")
                            stat_last = P.emit("vector", lambda e, rg=rg: e.tensor_copy(
                                qacc[:, 512 * rg:512 * (rg + 1)], sqt[:]),
                                signal=(rg == 3), tag="cq0")
                        else:
                            P.emit("vector", lambda e, yb=yb, rg=rg: e.tensor_tensor(
                                sacc[:, 512 * rg:512 * (rg + 1)],
                                sacc[:, 512 * rg:512 * (rg + 1)], yb[:], ALU.add), tag="cs")
                            P.emit("vector", lambda e, yb=yb: e.tensor_tensor(
                                sqt[:], yb[:], yb[:], ALU.mult), tag="cm")
                            stat_last = P.emit("vector", lambda e, rg=rg: e.tensor_tensor(
                                qacc[:, 512 * rg:512 * (rg + 1)],
                                qacc[:, 512 * rg:512 * (rg + 1)], sqt[:], ALU.add),
                                signal=(ht == NT - 1), tag="cq")
                    w_use[wpar] = P.ev("tensor")
                for c8 in range(8):
                    ln_stats(P, T, sacc[:, 256 * c8:256 * (c8 + 1)],
                             qacc[:, 256 * c8:256 * (c8 + 1)], ps_st, ps_bc, D,
                             bc2i[:, 256 * c8:256 * (c8 + 1)],
                             bc2n[:, 256 * c8:256 * (c8 + 1)], [stat_last])
                P.barrier()

            run_phase("c", _phase_c)
            with ExitStack() as ph:
                bc3i = sb(ph, "bc3i", [128, 2048], F32)
                bc3n = sb(ph, "bc3n", [128, 2048], F32)
                def _phase_d():
                 with ExitStack() as ph2:
                    ybig = sb(ph2, "d_y", [128, NT, 512], F32)
                    xbig = sb(ph2, "d_x", [128, NT, 512], F32)
                    zt = [sb(ph2, "d_z0", [128, 512], F32), sb(ph2, "d_z1", [128, 512], F32)]
                    tmp = sb(ph2, "d_t", [128, 512], F32)
                    sacc2 = sb(ph2, "d_sa", [128, 2048], F32)
                    qacc2 = sb(ph2, "d_qa", [128, 2048], F32)
                    ps_st = pt(ph2, "d_st", [128, 512])
                    ps_bc = pt(ph2, "d_bc", [128, 512])

                    y_ev = None
                    z_st = {0: None, 1: None}
                    stat_last = None
                    for cg in range(4):
                        vprev = P.ev("vector")
                        y_ev = P.emit("sync", lambda e, cg=cg: e.dma_start(
                            out=ybig[:], in_=y_sc[cg]), after=[vprev], signal=True, tag=f"dy{cg}")
                        P.emit("sync", lambda e, cg=cg: e.dma_start(
                            out=xbig[:, :, 0:256], in_=x_cm[CHUNKS[2 * cg]]),
                            after=[vprev], signal=True, tag=f"dxa{cg}")
                        x_ev = P.emit("sync", lambda e, cg=cg: e.dma_start(
                            out=xbig[:, :, 256:512], in_=x_cm[CHUNKS[2 * cg + 1]]),
                            signal=True, tag=f"dxb{cg}")
                        for t in range(NT):
                            tp = t % 2
                            z = zt[tp]
                            cols = slice(512 * cg, 512 * (cg + 1))
                            P.emit("vector", lambda e, t=t, cols=cols: e.tensor_tensor(
                                tmp[:], ybig[:, t, :], bc2i[:, cols], ALU.mult),
                                after=([y_ev] if t == 0 else ()), tag="d1")
                            P.emit("vector", lambda e, cols=cols: e.tensor_tensor(
                                tmp[:], tmp[:], bc2n[:, cols], ALU.subtract), tag="d2")
                            ez = P.emit("vector", lambda e, t=t, z=z: e.tensor_tensor(
                                z[:], xbig[:, t, :], tmp[:], ALU.add),
                                after=[x_ev if t == 0 else None, z_st[tp]],
                                signal=True, tag="d3")
                            z_st[tp] = P.emit("gpsimd", lambda e, cg=cg, t=t, z=z: e.dma_start(
                                out=z_sc[cg, :, t, :], in_=z[:]), after=[ez], signal=True,
                                tag=f"dz{cg}_{t}")
                            if cg == 0 and t == 0:
                                P.emit("vector", lambda e, z=z, cols=cols: e.tensor_copy(
                                    sacc2[:, cols], z[:]), tag="ds0")
                                P.emit("vector", lambda e, z=z: e.tensor_tensor(
                                    tmp[:], z[:], z[:], ALU.mult), tag="dm0")
                                stat_last = P.emit("vector", lambda e, cols=cols: e.tensor_copy(
                                    qacc2[:, cols], tmp[:]), tag="dq0")
                            elif t == 0:
                                P.emit("vector", lambda e, z=z, cols=cols: e.tensor_copy(
                                    sacc2[:, cols], z[:]), tag="ds0b")
                                P.emit("vector", lambda e, z=z: e.tensor_tensor(
                                    tmp[:], z[:], z[:], ALU.mult), tag="dm0b")
                                stat_last = P.emit("vector", lambda e, cols=cols: e.tensor_copy(
                                    qacc2[:, cols], tmp[:]), tag="dq0b")
                            else:
                                P.emit("vector", lambda e, z=z, cols=cols: e.tensor_tensor(
                                    sacc2[:, cols], sacc2[:, cols], z[:], ALU.add), tag="ds")
                                P.emit("vector", lambda e, z=z: e.tensor_tensor(
                                    tmp[:], z[:], z[:], ALU.mult), tag="dm")
                                stat_last = P.emit("vector", lambda e, cols=cols: e.tensor_tensor(
                                    qacc2[:, cols], qacc2[:, cols], tmp[:], ALU.add),
                                    signal=(t == NT - 1), tag="dq")
                    for c8 in range(8):
                        ln_stats(P, T, sacc2[:, 256 * c8:256 * (c8 + 1)],
                                 qacc2[:, 256 * c8:256 * (c8 + 1)], ps_st, ps_bc, D,
                                 bc3i[:, 256 * c8:256 * (c8 + 1)],
                                 bc3n[:, 256 * c8:256 * (c8 + 1)], [stat_last])
                    P.barrier()

                run_phase("d", _phase_d)
                def _phase_d2():
                 with ExitStack() as ph2:
                    zb = sb(ph2, "d2_z", [128, NT, 256], F32)
                    x2b = [sb(ph2, "d2_x0", [128, NT, 256], BF16),
                           sb(ph2, "d2_x1", [128, NT, 256], BF16)]
                    tmp = sb(ph2, "d2_t", [128, 256], F32)
                    x2st = [None, None]
                    for cc2 in range(8):
                        cg, hf = divmod(cc2, 2)
                        par = cc2 % 2
                        xo = x2b[par]
                        vprev = P.ev("vector")
                        zev = None
                        for t in range(NT):
                            zev = P.emit("sync", lambda e, cg=cg, hf=hf, t=t: e.dma_start(
                                out=zb[:, t, :], in_=z_sc[cg, :, t, 256 * hf:256 * (hf + 1)]),
                                after=([vprev] if t == 0 else ()), signal=True, tag=f"d2z{cc2}_{t}")
                        cols = slice(256 * cc2, 256 * (cc2 + 1))
                        aev = None
                        for t in range(NT):
                            P.emit("vector", lambda e, t=t, cols=cols: e.tensor_tensor(
                                tmp[:], zb[:, t, :], bc3i[:, cols], ALU.mult),
                                after=([zev, x2st[par]] if t == 0 else ()), tag="d21")
                            aev = P.emit("vector", lambda e, t=t, cols=cols, xo=xo: e.tensor_tensor(
                                xo[:, t, :], tmp[:], bc3n[:, cols], ALU.subtract),
                                signal=(t == NT - 1), tag="d22")
                        x2st[par] = P.emit("sync", lambda e, cc2=cc2, xo=xo: e.dma_start(
                            out=x2_sc[cc2], in_=xo[:]), after=[aev], signal=True, tag=f"d2s{cc2}")
                    P.barrier()

                run_phase("d2", _phase_d2)
        def _phase_e():
         with ExitStack() as ph:
            xres = sb(ph, "e_x", [128, NT, 2048], BF16)
            qgb = [sb(ph, "e_qg0", [128, D], I8), sb(ph, "e_qg1", [128, D], I8)]
            qub = [sb(ph, "e_qu0", [128, D], I8), sb(ph, "e_qu1", [128, D], I8)]
            deqb_g = sb(ph, "e_dqg", [128, D], BF16)
            deqb_u = sb(ph, "e_dqu", [128, D], BF16)
            lhg = [sb(ph, "e_lg0", [128, D], BF16), sb(ph, "e_lg1", [128, D], BF16)]
            lhu0 = sb(ph, "e_lu0", [128, D], BF16)
            lhu = [lhu0, lhu0]
            scb = [sb(ph, "e_sc0", [128, 4, 32], F32), sb(ph, "e_sc1", [128, 4, 32], F32)]
            zsb = [sb(ph, "e_zs0", [128, 2, 32], F32), sb(ph, "e_zs1", [128, 2, 32], F32)]
            sgt = sb(ph, "e_sg", [128, 512], BF16)
            hbuf = sb(ph, "e_h", [128, 2048], BF16)
            ps_g = [pt(ph, "e_g0", [128, 512]), pt(ph, "e_g1", [128, 512])]
            ps_u = [pt(ph, "e_u0", [128, 512]), pt(ph, "e_u1", [128, 512])]
            ps_t = [pt(ph, "e_t0", [128, 1024], BF16), pt(ph, "e_t1", [128, 1024], BF16)]

            for s_ in range(8):
                P.emit("sync", lambda e, s_=s_: e.dma_start(
                    out=xres[:, :, 256 * s_:256 * (s_ + 1)], in_=x2_sc[s_]),
                    signal=True, tag=f"ex{s_}")
            xall = P.ev("sync")

            deq_g_hist = {0: None, 1: None}
            deq_u_hist = {0: None, 1: None}
            mm_hist = {0: None, 1: None}
            mm_prev = None
            trg_pe_prev = None
            tru_pe_prev = None
            tr_free = {0: None, 1: None}
            mm_free = {"g": {0: None, 1: None}, "u": {0: None, 1: None}}
            sil_done = {0: None, 1: None}
            h_st = {}
            for isl in range(NISL):
                par = isl % 2
                scp = scb[par]
                zsp = zsb[par]
                qg_ev = P.emit("sync", lambda e, isl=isl, b=qgb[par]: e.dma_start(
                    out=b[:], in_=qwg_cm[isl]), after=[deq_g_hist[par]],
                    signal=True, tag=f"eqg{isl}")
                qu_ev = P.emit("sync", lambda e, isl=isl, b=qub[par]: e.dma_start(
                    out=b[:], in_=qwu_cm[isl]), after=[deq_u_hist[par]],
                    signal=True, tag=f"equ{isl}")
                sc_ev = P.emit("sync", lambda e, isl=isl, s=scp: e.dma_start(
                    out=s[:], in_=gu_sc[isl]),
                    after=[deq_g_hist[par], deq_u_hist[par]], signal=True, tag=f"esc{isl}")
                ezs = P.emit("vector", lambda e, s=scp, z=zsp: e.scalar_tensor_tensor(
                    z[:, 0, :], s[:, 1, :], -1.0, s[:, 0, :], ALU.mult, ALU.mult),
                    after=[sc_ev, deq_g_hist[par]], signal=True, tag="ezs_g")
                ezs2 = P.emit("vector", lambda e, s=scp, z=zsp: e.scalar_tensor_tensor(
                    z[:, 1, :], s[:, 3, :], -1.0, s[:, 2, :], ALU.mult, ALU.mult),
                    after=[sc_ev, deq_u_hist[par]], signal=True, tag="ezs_u")
                dq = None
                for gi in range(NT):
                    dq = P.emit("scalar", lambda e, gi=gi, s=scp, z=zsp, b=qgb[par]: e.activation(
                        deqb_g[:, 128 * gi:128 * (gi + 1)], b[:, 128 * gi:128 * (gi + 1)],
                        AF.Identity, scale=s[:, 0, gi:gi + 1], bias=z[:, 0, gi:gi + 1]),
                        after=([qg_ev, ezs, trg_pe_prev] if gi == 0 else ()),
                        signal=(gi % 8 == 7), tag=f"edqg{isl}_{gi}")
                deq_g_done = dq
                deq_g_hist[par] = dq
                for bb in range(4):
                    tpar = bb % 2
                    ptr = ps_t[tpar]
                    for j in range(8):
                        t = 8 * bb + j
                        etr = P.emit("tensor", lambda e, t=t, j=j, p=ptr: e.transpose(
                            p[:, 128 * j:128 * (j + 1)], deqb_g[:, 128 * t:128 * (t + 1)],
                            T["ident"][:]),
                            after=([deq_g_done, tr_free[tpar], eid] if j == 0 else ()),
                            signal=(j == 7), tag=f"etrg{isl}_{bb}")
                    tr_free[tpar] = P.emit("scalar", lambda e, bb=bb, p=ptr, o=lhg[par]: e.copy(
                        o[:, 1024 * bb:1024 * (bb + 1)], p[:]),
                        after=[etr, mm_hist[par]], signal=True, tag=f"etge{isl}_{bb}")
                trg_pe_prev = etr
                ev_lhg = [tr_free[0], tr_free[1]]
                dq = None
                for gi in range(NT):
                    dq = P.emit("vector", lambda e, gi=gi, s=scp, z=zsp, b=qub[par]: e.tensor_scalar(
                        deqb_u[:, 128 * gi:128 * (gi + 1)], b[:, 128 * gi:128 * (gi + 1)],
                        s[:, 2, gi:gi + 1], z[:, 1, gi:gi + 1], ALU.mult, ALU.add),
                        after=([qu_ev, ezs2, tru_pe_prev] if gi == 0 else ()),
                        signal=(gi % 8 == 7), tag=f"edqu{isl}_{gi}")
                deq_u_done = dq
                deq_u_hist[par] = dq
                for bb in range(4):
                    tpar = bb % 2
                    ptr = ps_t[tpar]
                    for j in range(8):
                        t = 8 * bb + j
                        etr = P.emit("tensor", lambda e, t=t, j=j, p=ptr: e.transpose(
                            p[:, 128 * j:128 * (j + 1)], deqb_u[:, 128 * t:128 * (t + 1)],
                            T["ident"][:]),
                            after=([deq_u_done, tr_free[tpar]] if j == 0 else ()),
                            signal=(j == 7), tag=f"etru{isl}_{bb}")
                    tr_free[tpar] = P.emit("vector", lambda e, bb=bb, p=ptr, o=lhu[par]: e.tensor_copy(
                        o[:, 1024 * bb:1024 * (bb + 1)], p[:]),
                        after=[etr, mm_prev], signal=True, tag=f"etue{isl}_{bb}")
                tru_pe_prev = etr
                ev_lhu = [tr_free[0], tr_free[1]]
                for rg in range(4):
                    pp = rg % 2
                    pg, pu = ps_g[pp], ps_u[pp]
                    for t in range(NT):
                        eg = P.emit("tensor", lambda e, t=t, rg=rg, p=pg, w=lhg[par]: e.matmul(
                            p[:], w[:, 128 * t:128 * (t + 1)],
                            xres[:, t, 512 * rg:512 * (rg + 1)],
                            start=(t == 0), stop=(t == NT - 1)),
                            after=(ev_lhg + [xall, mm_free["g"][pp]] if t == 0 else ()),
                            signal=(t == NT - 1), tag=f"emg{isl}_{rg}")
                    for t in range(NT):
                        eu = P.emit("tensor", lambda e, t=t, rg=rg, p=pu, w=lhu[par]: e.matmul(
                            p[:], w[:, 128 * t:128 * (t + 1)],
                            xres[:, t, 512 * rg:512 * (rg + 1)],
                            start=(t == 0), stop=(t == NT - 1)),
                            after=(ev_lhu + [mm_free["u"][pp]] if t == 0 else ()),
                            signal=(t == NT - 1), tag=f"emu{isl}_{rg}")
                    esil = P.emit("scalar", lambda e, p=pg: e.activation(
                        sgt[:], p[:], AF.Silu), after=[eg, sil_done[pp]], signal=True,
                        tag=f"esil{isl}_{rg}")
                    mm_free["g"][pp] = esil
                    eh = P.emit("vector", lambda e, rg=rg, p=pu: e.tensor_tensor(
                        hbuf[:, 512 * rg:512 * (rg + 1)], p[:], sgt[:], ALU.mult),
                        after=[eu, esil, h_st.get(rg)], signal=True, tag=f"eh{isl}_{rg}")
                    mm_free["u"][pp] = eh
                    sil_done[pp] = eh
                    h_st[rg] = P.emit("gpsimd", lambda e, isl=isl, rg=rg: e.dma_start(
                        out=h_sc[rg, :, isl, :], in_=hbuf[:, 512 * rg:512 * (rg + 1)]),
                        after=[eh], signal=True, tag=f"ehst{isl}_{rg}")
                mm_hist[par] = P.ev("tensor")
                mm_prev = mm_hist[par]
            P.barrier()

        run_phase("e", _phase_e)
        def _phase_f0():
         with ExitStack() as ph:
            qdb = [sb(ph, "f0_q0", [128, II], I8), sb(ph, "f0_q1", [128, II], I8)]
            deqd = sb(ph, "f0_dq", [128, II], BF16)
            stg = [sb(ph, "f0_s0", [128, II], BF16), sb(ph, "f0_s1", [128, II], BF16)]
            dscb = [sb(ph, "f0_c0", [128, 2, NISL], F32), sb(ph, "f0_c1", [128, 2, NISL], F32)]
            zsd = sb(ph, "f0_zs", [128, NISL], F32)
            ps_t = [pt(ph, "f0_t0", [128, 1024], BF16), pt(ph, "f0_t1", [128, 1024], BF16)]

            q_ev = [None, None]
            qfree = [[], []]
            deq_done = None
            tr_free = {0: None, 1: None}
            st_ev = [None, None]
            zs_used = []
            for ot in range(NT):
                par = ot % 2
                qd = qdb[par]
                dc = dscb[par]
                q_ev[par] = P.emit("sync", lambda e, ot=ot, q=qd: e.dma_start(
                    out=q[:], in_=qwd_cm[ot]), after=qfree[par], signal=True, tag=f"f0q{ot}")
                sc_ev = P.emit("sync", lambda e, ot=ot, d=dc: e.dma_start(
                    out=d[:], in_=d_sc[ot]), after=qfree[par], signal=True, tag=f"f0c{ot}")
                ezs = P.emit("vector", lambda e, d=dc: e.scalar_tensor_tensor(
                    zsd[:], d[:, 1, :], -1.0, d[:, 0, :], ALU.mult, ALU.mult),
                    after=[sc_ev] + zs_used, signal=True, tag="f0zs")
                dq = None
                for gi in range(NISL):
                    eng = "scalar" if (gi % 2 == 0) else "vector"
                    if eng == "scalar":
                        dq = P.emit("scalar", lambda e, gi=gi, q=qd, d=dc: e.activation(
                            deqd[:, 128 * gi:128 * (gi + 1)], q[:, 128 * gi:128 * (gi + 1)],
                            AF.Identity, scale=d[:, 0, gi:gi + 1], bias=zsd[:, gi:gi + 1]),
                            after=([q_ev[par], ezs, deq_done] if gi <= 1 else ()),
                            signal=(gi == NISL - 2), tag=f"f0dq{ot}_{gi}")
                    else:
                        dq2 = P.emit("vector", lambda e, gi=gi, q=qd, d=dc: e.tensor_scalar(
                            deqd[:, 128 * gi:128 * (gi + 1)], q[:, 128 * gi:128 * (gi + 1)],
                            d[:, 0, gi:gi + 1], zsd[:, gi:gi + 1], ALU.mult, ALU.add),
                            after=([q_ev[par], ezs, deq_done] if gi <= 1 else ()),
                            signal=(gi == NISL - 1), tag=f"f0dqv{ot}_{gi}")
                deq_s, deq_v = dq, dq2
                zs_used = [deq_s, deq_v]
                qfree[par] = [deq_s, deq_v]
                sg = stg[par]
                nbb = (NISL + 7) // 8
                for bb in range(nbb):
                    tpar = bb % 2
                    ptr = ps_t[tpar]
                    n_in = min(8, NISL - 8 * bb)
                    for j in range(n_in):
                        t = 8 * bb + j
                        etr = P.emit("tensor", lambda e, t=t, j=j, p=ptr: e.transpose(
                            p[:, 128 * j:128 * (j + 1)], deqd[:, 128 * t:128 * (t + 1)],
                            T["ident"][:]),
                            after=([deq_s, deq_v, tr_free[tpar], eid] if j == 0 else ()),
                            signal=(j == n_in - 1), tag=f"f0tr{ot}_{bb}")
                    eng = "scalar" if (bb % 2 == 0) else "vector"
                    tr_free[tpar] = P.emit(eng, lambda e, bb=bb, n_in=n_in, p=ptr, sg=sg:
                        (e.copy if hasattr(e, 'copy') else e.tensor_copy)(
                        sg[:, 1024 * bb:1024 * bb + 128 * n_in], p[:, 0:128 * n_in]),
                        after=[etr, st_ev[par]], signal=True, tag=f"f0ev{ot}_{bb}")
                deq_done = tr_free[(nbb - 1) % 2]
                st_ev[par] = P.emit("gpsimd", lambda e, ot=ot, sg=sg: e.dma_start(
                    out=wd_sc[ot], in_=sg[:].rearrange("p (a b) -> p a b", a=NISL)),
                    after=[tr_free[0], tr_free[1]], signal=True, tag=f"f0st{ot}")
            P.barrier()

        run_phase("f0", _phase_f0)
        def _phase_f():
         with ExitStack() as ph:
            bc4i = sb(ph, "f_bci", [128, 512], F32)
            bc4n = sb(ph, "f_bcn", [128, 512], F32)
            hres = sb(ph, "f_h", [128, NISL, 512], BF16)
            wdb = [sb(ph, "f_w0", [128, NISL, 128], BF16), sb(ph, "f_w1", [128, NISL, 128], BF16)]
            o2b = [sb(ph, "f_o0", [128, 512], F32), sb(ph, "f_o1", [128, 512], F32)]
            sacc = sb(ph, "f_sa", [128, 512], F32)
            qacc = sb(ph, "f_qa", [128, 512], F32)
            sqt = sb(ph, "f_sq", [128, 512], F32)
            ztl = [sb(ph, "f_z0", [128, 512], F32), sb(ph, "f_z1", [128, 512], F32)]
            otl = [sb(ph, "f_t0", [128, 512], F32), sb(ph, "f_t1", [128, 512], F32)]
            oub = [sb(ph, "f_u0", [128, 512], F32), sb(ph, "f_u1", [128, 512], F32)]
            ps_f = [pt(ph, "f_p0", [128, 512]), pt(ph, "f_p1", [128, 512])]
            ps_st = pt(ph, "f_st", [128, 512])
            ps_bc = pt(ph, "f_bc", [128, 512])

            h_ev = None
            h_use = None
            w_ev = [None, None]
            w_use = [None, None]
            evac = {0: None, 1: None}
            o_st = {0: None, 1: None}
            ap_done = None
            out_st = {0: None, 1: None}
            zld = {0: None, 1: None}
            old = {0: None, 1: None}
            for rc in range(4):
                h_ev = P.emit("sync", lambda e, rc=rc: e.dma_start(
                    out=hres[:], in_=h_sc[rc]), after=[h_use], signal=True, tag=f"fh{rc}")
                stat_last = None
                for ot in range(NT):
                    wpar = ot % 2
                    wb = wdb[wpar]
                    w_ev[wpar] = P.emit("sync", lambda e, ot=ot, w=wb: e.dma_start(
                        out=w[:], in_=wd_sc[ot]), after=[w_use[wpar]], signal=True, tag=f"fw{rc}_{ot}")
                    pp = ot % 2
                    pf = ps_f[pp]
                    ob = o2b[pp]
                    for it in range(NISL):
                        evm = P.emit("tensor", lambda e, it=it, w=wb, p=pf: e.matmul(
                            p[:], w[:, it, :], hres[:, it, :],
                            start=(it == 0), stop=(it == NISL - 1)),
                            after=([h_ev, w_ev[wpar], evac[pp]] if it == 0 else ()),
                            signal=(it == NISL - 1), tag=f"fmm{rc}_{ot}")
                    ec = P.emit("vector", lambda e, ob=ob, p=pf: e.tensor_copy(ob[:], p[:]),
                                after=[evm, o_st[pp]], signal=True, tag=f"fcp{rc}_{ot}")
                    evac[pp] = ec
                    o_st[pp] = P.emit("gpsimd", lambda e, rc=rc, ot=ot, ob=ob: e.dma_start(
                        out=o2_sc[rc, :, ot, :], in_=ob[:]), after=[ec], signal=True,
                        tag=f"fst{rc}_{ot}")
                    if ot == 0:
                        P.emit("vector", lambda e, ob=ob: e.tensor_copy(sacc[:], ob[:]),
                               after=[_LN_STATE["sum_mm"]], tag="fs0")
                        P.emit("vector", lambda e, ob=ob: e.tensor_tensor(
                            sqt[:], ob[:], ob[:], ALU.mult), tag="fm0")
                        stat_last = P.emit("vector", lambda e: e.tensor_copy(qacc[:], sqt[:]), tag="fq0")
                    else:
                        P.emit("vector", lambda e, ob=ob: e.tensor_tensor(
                            sacc[:], sacc[:], ob[:], ALU.add), tag="fs")
                        P.emit("vector", lambda e, ob=ob: e.tensor_tensor(
                            sqt[:], ob[:], ob[:], ALU.mult), tag="fm")
                        stat_last = P.emit("vector", lambda e: e.tensor_tensor(
                            qacc[:], qacc[:], sqt[:], ALU.add),
                            signal=(ot == NT - 1), tag="fq")
                    w_use[wpar] = P.ev("tensor")
                h_use = P.ev("tensor")
                for c2 in range(2):
                    ln_stats(P, T, sacc[:, 256 * c2:256 * (c2 + 1)],
                             qacc[:, 256 * c2:256 * (c2 + 1)], ps_st, ps_bc, D,
                             bc4i[:, 256 * c2:256 * (c2 + 1)],
                             bc4n[:, 256 * c2:256 * (c2 + 1)], [stat_last])
                bc_ev = [P.ev("scalar")]
                for t in range(NT):
                    tp = t % 2
                    z = ztl[tp]
                    o2t = otl[tp]
                    ou = oub[tp]
                    ez = P.emit("sync", lambda e, rc=rc, t=t, z=z: e.dma_start(
                        out=z[:], in_=z_sc[rc, :, t, :]), after=[ap_done], signal=True,
                        tag=f"fz{rc}_{t}")
                    eo = P.emit("sync", lambda e, rc=rc, t=t, o=o2t: e.dma_start(
                        out=o[:], in_=o2_sc[rc, :, t, :]),
                        after=([o_st[0], o_st[1], ap_done] if t == 0 else [ap_done]),
                        signal=True, tag=f"fo{rc}_{t}")
                    P.emit("vector", lambda e, o=o2t: e.tensor_tensor(
                        sqt[:], o[:], bc4i[:], ALU.mult),
                        after=[eo] + bc_ev + [out_st[tp]], tag="fa1")
                    P.emit("vector", lambda e: e.tensor_tensor(
                        sqt[:], sqt[:], bc4n[:], ALU.subtract), tag="fa2")
                    ap_done = P.emit("vector", lambda e, z=z, ou=ou: e.tensor_tensor(
                        ou[:], z[:], sqt[:], ALU.add), after=[ez], signal=True, tag="fa3")
                    out_st[tp] = P.emit("gpsimd", lambda e, rc=rc, t=t, ou=ou: e.dma_start(
                        out=out_cm[rc, :, t, :], in_=ou[:]), after=[ap_done], signal=True,
                        tag=f"fout{rc}_{t}")
            P.barrier()

        run_phase("f", _phase_f)
        with nc.Block() as block:
            P.build(block)
    return nc



_NC_CACHE = {}


def _get_nc(taps=()):
    key = tuple(sorted(taps))
    if key not in _NC_CACHE:
        _NC_CACHE[key] = build_nc(taps=key)
    return _NC_CACHE[key]


def _rope_tables():
    pos = np.arange(S, dtype=np.float32)[:, None]
    freqs = np.exp(-np.arange(0, HD, 2, dtype=np.float32) * (math.log(10000.0) / HD))
    ang = pos * freqs
    return np.ascontiguousarray(np.cos(ang).T), np.ascontiguousarray(np.sin(ang).T)


def make_in_maps(inputs):
    x = np.asarray(inputs["L0_attn_hidden_states"], dtype=np.float32)
    Wq = np.asarray(inputs["L0_attn_Wq"], dtype=np.float32)
    Wk = np.asarray(inputs["L0_attn_Wk"], dtype=np.float32)
    Wv = np.asarray(inputs["L0_attn_Wv"], dtype=np.float32)
    Wo = np.asarray(inputs["L0_attn_Wo"], dtype=np.float32)
    qwg = np.asarray(inputs["L0_ffn_qw_gate"], dtype=np.int32)
    qwu = np.asarray(inputs["L0_ffn_qw_up"], dtype=np.int32)
    qwd = np.asarray(inputs["L0_ffn_qw_down"], dtype=np.int32)
    sg = np.asarray(inputs["L0_ffn_s_gate"], dtype=np.float32)
    zg = np.asarray(inputs["L0_ffn_z_gate"], dtype=np.float32)
    su = np.asarray(inputs["L0_ffn_s_up"], dtype=np.float32)
    zu = np.asarray(inputs["L0_ffn_z_up"], dtype=np.float32)
    sd = np.asarray(inputs["L0_ffn_s_down"], dtype=np.float32)
    zd = np.asarray(inputs["L0_ffn_z_down"], dtype=np.float32)

    wk_cm = np.ascontiguousarray(Wk.T.reshape(NT, 128, 1024).transpose(1, 0, 2))
    wv_cm = np.ascontiguousarray(Wv.T.reshape(NT, 128, 1024).transpose(1, 0, 2))
    wq_cm = np.ascontiguousarray(Wq.reshape(NT, 128, NT, 128).transpose(0, 3, 2, 1))
    wo_cm = np.ascontiguousarray(Wo.reshape(NT, 128, NT, 128).transpose(0, 3, 2, 1))
    qwg_cm = np.ascontiguousarray(qwg.reshape(NISL, 128, D).astype(np.int8))
    qwu_cm = np.ascontiguousarray(qwu.reshape(NISL, 128, D).astype(np.int8))
    qwd_cm = np.ascontiguousarray(qwd.reshape(NT, 128, II).astype(np.int8))
    gu_scm = np.ascontiguousarray(np.stack(
        [sg.reshape(NISL, 128, 32), zg.reshape(NISL, 128, 32),
         su.reshape(NISL, 128, 32), zu.reshape(NISL, 128, 32)], axis=2))
    d_scm = np.ascontiguousarray(np.stack(
        [sd.reshape(NT, 128, NISL), zd.reshape(NT, 128, NISL)], axis=2))
    cosT, sinT = _rope_tables()
    rope_k = np.ascontiguousarray(np.stack([cosT, sinT], axis=0))
    ident = np.eye(128, dtype=ml_dtypes.bfloat16)

    in_maps = []
    for core in range(8):
        b = core // 2
        chunks = EVEN_CHUNKS if core % 2 == 0 else ODD_CHUNKS
        own_cols = np.concatenate([np.arange(256 * c, 256 * (c + 1)) for c in chunks])
        truepos = np.arange(NCC)
        if core % 2 == 1:
            for s_ in range(8):
                truepos[EVEN_CHUNKS[s_]] = ODD_CHUNKS[s_]
                truepos[ODD_CHUNKS[s_]] = EVEN_CHUNKS[s_]
        x_cm = np.ascontiguousarray(
            x[b].reshape(NCC, 256, NT, 128).transpose(0, 3, 2, 1)[truepos])
        rope_q = np.ascontiguousarray(
            np.stack([cosT[:, own_cols], sinT[:, own_cols]], axis=0))
        ck = cosT.reshape(64, NCC, 256)[:, truepos, :].reshape(64, S)
        sk = sinT.reshape(64, NCC, 256)[:, truepos, :].reshape(64, S)
        rope_k = np.ascontiguousarray(np.stack([ck, sk], axis=0))
        masks = np.zeros((8, 128, 1024), dtype=np.float32)
        for s_ in range(8):
            c = chunks[s_]
            n_kt = 4 * (s_ + 1)
            for j in range(4):
                kt = n_kt - 4 + j
                unit = kt // 2
                true_k = 256 * truepos[unit] + (kt % 2) * 128 + np.arange(128)[:, None]
                gq = 256 * c + np.arange(256)[None, :]
                masks[s_, :, 256 * j:256 * (j + 1)] = np.where(true_k <= gq, 0.0, -1e9)
        in_maps.append({
            "x_cm": x_cm, "wk_cm": wk_cm, "wv_cm": wv_cm, "wq_cm": wq_cm,
            "wo_cm": wo_cm, "qwg_cm": qwg_cm, "qwu_cm": qwu_cm, "qwd_cm": qwd_cm,
            "gu_sc": gu_scm, "d_sc": d_scm, "rope_q": rope_q, "rope_k": rope_k,
            "masks_in": masks.astype(ml_dtypes.bfloat16), "ident_in": ident,
        })
    return in_maps


def assemble_output(results):
    out = np.zeros((B, S, D), dtype=np.float32)
    for core in range(8):
        b = core // 2
        chunks = EVEN_CHUNKS if core % 2 == 0 else ODD_CHUNKS
        own_cols = np.concatenate([np.arange(256 * c, 256 * (c + 1)) for c in chunks])
        oc = np.asarray(results[core]["out_cm"], dtype=np.float32)
        outT = oc.transpose(2, 1, 0, 3).reshape(D, 2048)
        out[b, own_cols, :] = outT.T
    return out


def kernel(**inputs):
    nc = _get_nc()
    in_maps = make_in_maps(inputs)
    res = run_bass_kernel_spmd(nc, in_maps, core_ids=list(range(8)))
    return assemble_output(res.results)


# version 13
# speedup vs baseline: 1.0321x; 1.0321x over previous
import math
from contextlib import ExitStack

import numpy as np
import ml_dtypes

import concourse.bass as bass
import concourse.mybir as mybir
from concourse.bass_utils import run_bass_kernel_spmd

F32 = mybir.dt.float32
BF16 = mybir.dt.bfloat16
I32 = mybir.dt.int32
I8 = mybir.dt.int8
AF = mybir.ActivationFunctionType
ALU = mybir.AluOpType

B, S, D = 4, 4096, 4096
H, KV, HD = 32, 8, 128
II = 11008
NT = 32
NISL = II // 128
NCC = 16
EPS = 1e-5
ISQ = 1.0 / math.sqrt(HD)

EVEN_CHUNKS = sorted([4 * k for k in range(4)] + [4 * k + 3 for k in range(4)])
ODD_CHUNKS = sorted([4 * k + 1 for k in range(4)] + [4 * k + 2 for k in range(4)])
CHUNKS = EVEN_CHUNKS

ENGINES = ("sync", "gpsimd", "scalar", "vector", "tensor")


class Prog:

    def __init__(self, sems):
        self.sems = sems
        self.count = {k: 0 for k in ENGINES}
        self.steps = {k: [] for k in ENGINES}

    def emit(self, eng, fn, after=(), signal=False, tag=""):
        inc = 16 if eng in ("sync", "gpsimd") else 1
        wd = {}
        for ev in after:
            if ev is None:
                continue
            e2, v = ev
            if e2 == eng:
                continue
            assert v <= self.count[e2], f"future wait [{tag}]: {e2}@{v} > {self.count[e2]}"
            wd[e2] = max(wd.get(e2, 0), v)
        self.steps[eng].append((wd, fn, inc if signal else 0))
        if signal:
            self.count[eng] += inc
            return (eng, self.count[eng])
        return None

    def ev(self, eng):
        return (eng, self.count[eng])

    def barrier(self):
        evs = [self.ev(e) for e in ENGINES]
        for e in ENGINES:
            self.emit(e, None, after=evs, tag="barrier")

    def build(self, block):
        def mk(eng):
            def body(e):
                for wd, fn, inc in self.steps[eng]:
                    for e2, v in wd.items():
                        e.wait_ge(self.sems[e2], v)
                    if fn is not None:
                        inst = fn(e)
                        if inc:
                            inst.then_inc(self.sems[eng], inc)
            return body
        block.sync(mk("sync"))
        block.gpsimd(mk("gpsimd"))
        block.scalar(mk("scalar"))
        block.vector(mk("vector"))
        block.tensor(mk("tensor"))


_LN_STATE = {"var": None, "bc": None, "sum_mm": None}


def ln_stats(P, T, acc_s, acc_q, ps_st, ps_bc, n_feat, bc_inv, bc_nm, after_acc,
             direct=False):
    st_m, st_msq, st_var, st_std, st_inv, st_nm = T["stat6"]
    pv = _LN_STATE["var"]
    pb = _LN_STATE["bc"]
    e1 = P.emit("tensor", lambda e: e.matmul(
        ps_st[0:1, 0:256], T["ones_col"][:, 0:1], acc_s, start=True, stop=True),
        after=list(after_acc) + [pv], signal=True, tag="ln_sum")
    e2 = P.emit("tensor", lambda e: e.matmul(
        ps_st[0:1, 256:512], T["ones_col"][:, 0:1], acc_q, start=True, stop=True),
        after=list(after_acc), signal=True, tag="ln_sumsq")
    _LN_STATE["sum_mm"] = e2
    inv_n = 1.0 / n_feat
    P.emit("vector", lambda e: e.tensor_scalar_mul(st_m[:], ps_st[0:1, 0:256], float(inv_n)),
           after=[e1, e2], tag="ln_m")
    P.emit("vector", lambda e: e.tensor_tensor(
        st_msq[:], st_m[:], st_m[:], ALU.mult), tag="ln_msq")
    ev = P.emit("vector", lambda e: e.scalar_tensor_tensor(
        st_var[:], ps_st[0:1, 256:512], float(inv_n), st_msq[:], ALU.mult, ALU.subtract),
        after=[e2], signal=True, tag="ln_var")
    _LN_STATE["var"] = ev
    es = P.emit("scalar", lambda e: e.activation(
        st_std[:], st_var[:], AF.Sqrt, bias=np.float32(EPS)), after=[ev], signal=True, tag="ln_std")
    P.emit("vector", lambda e: e.reciprocal(st_inv[:], st_std[:]),
           after=[es], tag="ln_inv")
    ev2 = P.emit("vector", lambda e: e.tensor_tensor(
        st_nm[:], st_m[:], st_inv[:], ALU.mult), signal=True, tag="ln_nm")
    eb1 = P.emit("tensor", lambda e: e.matmul(
        ps_bc[:, 0:256], T["ones_row"][0:1, :], st_inv[:], start=True, stop=True),
        after=[ev2, pb], signal=True, tag="ln_bci")
    eb2 = P.emit("tensor", lambda e: e.matmul(
        ps_bc[:, 256:512], T["ones_row"][0:1, :], st_nm[:], start=True, stop=True),
        after=[ev2], signal=True, tag="ln_bcn")
    if direct:
        return eb1, eb2
    ec1 = P.emit("scalar", lambda e: e.copy(bc_inv, ps_bc[:, 0:256]),
                 after=[eb1, eb2] + list(after_acc), signal=True, tag="ln_cpi")
    ec2 = P.emit("scalar", lambda e: e.copy(bc_nm, ps_bc[:, 256:512]),
                 after=[eb2] + list(after_acc), signal=True, tag="ln_cpn")
    _LN_STATE["bc"] = ec2
    return ec1, ec2


PHASE_LIMIT = None


def build_nc(taps=()):
    _LN_STATE.update({"var": None, "bc": None, "sum_mm": None})
    nc = bass.Bass("TRN2", target_bir_lowering=False)

    def dram(name, shape, dtype, kind="ExternalInput"):
        return nc.dram_tensor(name, shape, dtype, kind=kind).ap()

    def scratch(name, shape, dtype):
        kind = "ExternalOutput" if name in taps else "Internal"
        return nc.dram_tensor(name, shape, dtype, kind=kind).ap()

    x_cm = dram("x_cm", [NCC, 128, NT, 256], F32)
    wk_cm = dram("wk_cm", [128, NT, 1024], F32)
    wv_cm = dram("wv_cm", [128, NT, 1024], F32)
    wq_cm = dram("wq_cm", [NT, 128, NT, 128], F32)
    wo_cm = dram("wo_cm", [NT, 128, NT, 128], F32)
    qwg_cm = dram("qwg_cm", [NISL, 128, D], I8)
    qwu_cm = dram("qwu_cm", [NISL, 128, D], I8)
    qwd_cm = dram("qwd_cm", [NT, 128, II], I8)
    gu_sc = dram("gu_sc", [NISL, 128, 4, 32], F32)
    d_sc = dram("d_sc", [NT, 128, 2, NISL], F32)
    rope_q = dram("rope_q", [2, 64, 2048], F32)
    rope_k = dram("rope_k", [2, 64, S], F32)
    masks_in = dram("masks_in", [8, 128, 1024], BF16)
    ident_in = dram("ident_in", [128, 128], BF16)
    out_cm = dram("out_cm", [4, 128, NT, 512], F32, kind="ExternalOutput")

    xln_sc = scratch("xln_sc", [NCC, 128, NT, 256], BF16)
    q_sc = scratch("q_sc", [8, 128, NT, 256], BF16)
    k_sc = scratch("k_sc", [128, KV, S], BF16)
    v_sc = scratch("v_sc", [128, NT, 1024], BF16)
    ao_sc = scratch("ao_sc", [8, 128, NT, 256], BF16)
    y_sc = scratch("y_sc", [4, 128, NT, 512], F32)
    z_sc = scratch("z_sc", [4, 128, NT, 512], F32)
    x2_sc = scratch("x2_sc", [8, 128, NT, 256], BF16)
    h_sc = scratch("h_sc", [4, 128, NISL, 512], BF16)
    wd_sc = scratch("wd_sc", [NT, 128, NISL, 128], BF16)
    o2_sc = scratch("o2_sc", [4, 128, NT, 512], F32)

    with ExitStack() as top:
        sems = {k: top.enter_context(nc.semaphore(f"sem_{k}")) for k in ENGINES}
        P = Prog(sems)

        def sb(st, name, shape, dt):
            return st.enter_context(nc.sbuf_tensor(name, shape, dt))

        def pt(st, name, shape, dt=F32):
            return st.enter_context(nc.psum_tensor(name, shape, dt))

        _stop = {}

        def run_phase(nm, fn):
            if _stop.get("s"):
                return
            fn()
            if PHASE_LIMIT is not None and nm == PHASE_LIMIT:
                _stop["s"] = True

        T = {}
        T["ones_col"] = sb(top, "ones_col", [128, 1], F32)
        T["ones_row"] = sb(top, "ones_row", [1, 128], F32)
        T["ident"] = sb(top, "ident", [128, 128], BF16)
        T["stat6"] = [sb(top, f"stat{i}", [1, 256], F32) for i in range(6)]

        eid = P.emit("sync", lambda e: e.dma_start(out=T["ident"][:], in_=ident_in),
                     signal=True, tag="ld_ident")
        P.emit("vector", lambda e: e.memset(T["ones_col"][:], 1.0), signal=True)
        P.emit("vector", lambda e: e.memset(T["ones_row"][:], 1.0), signal=True)

        def _phase_a():
         with ExitStack() as ph:
            xb = sb(ph, "a_x", [128, NT, 256], F32)
            lnb = [sb(ph, "a_l0", [128, NT, 256], BF16), sb(ph, "a_l1", [128, NT, 256], BF16)]
            wkb = sb(ph, "a_wk", [128, NT, 1024], BF16)
            wvb = sb(ph, "a_wv", [128, NT, 1024], BF16)
            acc_s = sb(ph, "a_as", [128, 256], F32)
            acc_q = sb(ph, "a_aq", [128, 256], F32)
            tmp = sb(ph, "a_tmp", [128, 256], F32)
            kout0 = sb(ph, "a_k0", [128, 256], BF16)
            kout = [kout0, kout0]
            vout0 = sb(ph, "a_v0", [128, 1024], BF16)
            vout = [vout0, vout0]
            cosk0 = sb(ph, "a_c0", [64, 256], F32)
            sink0 = sb(ph, "a_s0", [64, 256], F32)
            cosk = [cosk0, cosk0]
            sink = [sink0, sink0]
            t64a = tmp[0:64, :]
            t64b = sb(ph, "a_t64b", [64, 256], F32)[:]
            ps_st = pt(ph, "a_st", [128, 512])
            ps_bc = pt(ph, "a_bc", [128, 512])
            ps_k = [pt(ph, "a_pk0", [128, 512]), pt(ph, "a_pk1", [128, 512])]
            ps_v = [pt(ph, "a_pv0", [128, 512]), pt(ph, "a_pv1", [128, 512])]

            ewk = P.emit("gpsimd", lambda e: e.dma_start(out=wkb[:], in_=wk_cm),
                         signal=True, tag="a_wk")
            ewv = P.emit("gpsimd", lambda e: e.dma_start(out=wvb[:], in_=wv_cm),
                         signal=True, tag="a_wv")

            aev = None
            rope_used = [None, None]
            kv_mm_done = [None, None]
            pe_free = {0: None, 1: None}
            st_ev = {0: None, 1: None}
            ev_free = {0: None, 1: None}
            st2_ev = {0: None, 1: None}
            for cc in range(NCC):
                par = cc % 2
                lb = lnb[par]
                ck = cosk[par]
                sk = sink[par]
                rc_ev = P.emit("sync", lambda e, cc=cc, b=ck: e.dma_start(
                    out=b[:], in_=rope_k[0, :, 256 * cc:256 * (cc + 1)]),
                    after=[rope_used[1 - par]], signal=True, tag=f"a_rc{cc}")
                rs_ev = P.emit("sync", lambda e, cc=cc, b=sk: e.dma_start(
                    out=b[:], in_=rope_k[1, :, 256 * cc:256 * (cc + 1)]),
                    signal=True, tag=f"a_rs{cc}")
                x_ev0 = P.emit("sync", lambda e, cc=cc: e.dma_start(
                    out=xb[:, 0:16, :], in_=x_cm[cc, :, 0:16, :]), after=[aev],
                    signal=True, tag=f"a_x{cc}a")
                x_ev1 = P.emit("sync", lambda e, cc=cc: e.dma_start(
                    out=xb[:, 16:32, :], in_=x_cm[cc, :, 16:32, :]),
                    signal=True, tag=f"a_x{cc}b")
                for t in range(NT):
                    if t == 0:
                        P.emit("vector", lambda e: e.tensor_copy(acc_s[:], xb[:, 0, :]),
                               after=[x_ev0, _LN_STATE["sum_mm"]], tag="a_c")
                        P.emit("vector", lambda e: e.tensor_tensor(
                            acc_q[:], xb[:, 0, :], xb[:, 0, :], ALU.mult), tag="a_q0")
                    else:
                        P.emit("vector", lambda e, t=t: e.tensor_tensor(
                            acc_s[:], acc_s[:], xb[:, t, :], ALU.add),
                            after=([x_ev1] if t == 16 else ()), tag="a_s")
                        P.emit("vector", lambda e, t=t: e.tensor_tensor(
                            tmp[:], xb[:, t, :], xb[:, t, :], ALU.mult), tag="a_m")
                        last = P.emit("vector", lambda e: e.tensor_tensor(
                            acc_q[:], acc_q[:], tmp[:], ALU.add),
                            signal=(t == NT - 1), tag="a_q")
                e_inv, e_nm = ln_stats(P, T, acc_s[:], acc_q[:], ps_st, ps_bc, D,
                                       None, None, [last], direct=True)
                norm_evs = []
                for t in range(NT):
                    P.emit("vector", lambda e, t=t: e.tensor_tensor(
                        tmp[:], xb[:, t, :], ps_bc[:, 0:256], ALU.mult),
                        after=[e_inv, e_nm] if t == 0 else (), tag="a_p1")
                    aev = P.emit("vector", lambda e, o=lb, t=t: e.tensor_tensor(
                        o[:, t, :], tmp[:], ps_bc[:, 256:512], ALU.subtract),
                        after=([kv_mm_done[par]] if t == 0 else ()),
                        signal=(t % 8 == 7), tag="a_p2")
                    if t % 8 == 7:
                        norm_evs.append(aev)
                _LN_STATE["bc"] = aev
                if cc in CHUNKS:
                    P.emit("gpsimd", lambda e, cc=cc, b=lb: e.dma_start(
                        out=xln_sc[cc], in_=b[:]), after=[aev], signal=True,
                        tag=f"a_lnst{cc}")
                for ot in range(KV):
                    pp = ot % 2
                    pk = ps_k[pp]
                    for t in range(NT):
                        aft = ()
                        if ot == 0 and t % 8 == 0:
                            aft = [norm_evs[t // 8]] + ([ewk, pe_free[pp]] if t == 0 else [])
                        elif t == 0:
                            aft = [aev, ewk, pe_free[pp]]
                        evm = P.emit("tensor", lambda e, b=lb, t=t, ot=ot, p=pk: e.matmul(
                            p[:, 0:256], wkb[:, t, 128 * ot:128 * (ot + 1)], b[:, t, :],
                            start=(t == 0), stop=(t == NT - 1)),
                            after=aft,
                            signal=(t == NT - 1), tag=f"a_kmm{cc}_{ot}")
                    ko = kout[pp]
                    P.emit("vector", lambda e, p=pk, ck=ck: e.tensor_tensor(
                        t64a, p[0:64, 0:256], ck[:], ALU.mult),
                        after=[evm, rc_ev, rs_ev, st_ev[0], st_ev[1]], tag="a_r1")
                    P.emit("vector", lambda e, p=pk, sk=sk: e.tensor_tensor(
                        t64b, p[64:128, 0:256], sk[:], ALU.mult), tag="a_r2")
                    P.emit("vector", lambda e, ko=ko: e.tensor_tensor(
                        ko[0:64, :], t64a, t64b, ALU.subtract), tag="a_r3")
                    P.emit("vector", lambda e, p=pk, ck=ck: e.tensor_tensor(
                        t64a, p[64:128, 0:256], ck[:], ALU.mult), tag="a_r4")
                    P.emit("vector", lambda e, p=pk, sk=sk: e.tensor_tensor(
                        t64b, p[0:64, 0:256], sk[:], ALU.mult), tag="a_r5")
                    evr = P.emit("vector", lambda e, ko=ko: e.tensor_tensor(
                        ko[64:128, :], t64a, t64b, ALU.add),
                        signal=True, tag="a_r6")
                    pe_free[pp] = evr
                    st_ev[pp] = P.emit("gpsimd", lambda e, cc=cc, ot=ot, ko=ko: e.dma_start(
                        out=k_sc[:, ot, 256 * cc:256 * (cc + 1)], in_=ko[:]),
                        after=[evr], signal=True, tag=f"a_kst{cc}_{ot}")
                rope_used[par] = P.ev("vector")
                for j in range(2):
                    vo = vout[j]
                    for nh in range(2):
                        pv = ps_v[nh]
                        for t in range(NT):
                            evm = P.emit("tensor", lambda e, b=lb, t=t, j=j, nh=nh, p=pv: e.matmul(
                                p[:], b[:, t, 128 * j:128 * (j + 1)],
                                wvb[:, t, 512 * nh:512 * (nh + 1)],
                                start=(t == 0), stop=(t == NT - 1)),
                                after=([aev, ewv, ev_free[nh]] if t == 0 else ()),
                                signal=(t == NT - 1), tag=f"a_vmm{cc}_{j}_{nh}")
                        ev_free[nh] = P.emit("scalar", lambda e, vo=vo, nh=nh, p=pv: e.copy(
                            vo[:, 512 * nh:512 * (nh + 1)], p[:]),
                            after=[evm, st2_ev[0], st2_ev[1]], signal=True, tag=f"a_vcp{cc}_{j}_{nh}")
                    st2_ev[j] = P.emit("gpsimd", lambda e, cc=cc, j=j, vo=vo: e.dma_start(
                        out=v_sc[:, 2 * cc + j, :], in_=vo[:]),
                        after=[ev_free[0], ev_free[1]], signal=True, tag=f"a_vst{cc}_{j}")
                kv_mm_done[par] = P.ev("tensor")
            P.barrier()

        run_phase("a", _phase_a)

        def _phase_a3():
         with ExitStack() as ph:
            xres = sb(ph, "a3_x", [128, NT, 2048], BF16)
            wqb = [sb(ph, "a3_w0", [128, NT, 128], BF16), sb(ph, "a3_w1", [128, NT, 128], BF16)]
            qout = [sb(ph, "a3_q0", [128, 512], BF16), sb(ph, "a3_q1", [128, 512], BF16)]
            cosq = sb(ph, "a3_cos", [64, 2048], F32)
            sinq = sb(ph, "a3_sin", [64, 2048], F32)
            q64a = sb(ph, "a3_t64a", [64, 512], F32)
            q64b = sb(ph, "a3_t64b", [64, 512], F32)
            ps_q = [pt(ph, "a3_p0", [128, 512]), pt(ph, "a3_p1", [128, 512])]

            P.emit("sync", lambda e: e.dma_start(out=cosq[:], in_=rope_q[0]), signal=True)
            cq = P.emit("sync", lambda e: e.dma_start(out=sinq[:], in_=rope_q[1]), signal=True)
            for s_ in range(8):
                P.emit("sync", lambda e, s_=s_: e.dma_start(
                    out=xres[:, :, 256 * s_:256 * (s_ + 1)], in_=xln_sc[CHUNKS[s_]]),
                    signal=True, tag=f"a3x{s_}")
            xall = P.ev("sync")
            wq_ev = [None, None]
            use_w = [None, None]
            st_ev = {0: None, 1: None}
            pe_free = {0: None, 1: None}
            for ot in range(NT):
                wpar = ot % 2
                wb = wqb[wpar]
                wq_ev[wpar] = P.emit("gpsimd", lambda e, ot=ot, w=wb: e.dma_start(
                    out=w[:], in_=wq_cm[ot]), after=[use_w[wpar]], signal=True, tag=f"a3w{ot}")
                for rg in range(4):
                    pp = rg % 2
                    pq = ps_q[pp]
                    for t in range(NT):
                        evm = P.emit("tensor", lambda e, w=wb, t=t, rg=rg, p=pq: e.matmul(
                            p[:], w[:, t, :], xres[:, t, 512 * rg:512 * (rg + 1)],
                            start=(t == 0), stop=(t == NT - 1)),
                            after=([wq_ev[wpar], xall, pe_free[pp]] if t == 0 else ()),
                            signal=(t == NT - 1), tag=f"a3mm{ot}_{rg}")
                    qo = qout[pp]
                    c0, c1 = 512 * rg, 512 * (rg + 1)
                    P.emit("vector", lambda e, p=pq, c0=c0, c1=c1: e.tensor_tensor(
                        q64a[:], p[0:64, :], cosq[:, c0:c1], ALU.mult),
                        after=[evm, cq, st_ev[pp]], tag="a3r1")
                    P.emit("vector", lambda e, p=pq, c0=c0, c1=c1: e.tensor_tensor(
                        q64b[:], p[64:128, :], sinq[:, c0:c1], ALU.mult), tag="a3r2")
                    P.emit("vector", lambda e, qo=qo: e.tensor_tensor(
                        qo[0:64, :], q64a[:], q64b[:], ALU.subtract), tag="a3r3")
                    P.emit("vector", lambda e, p=pq, c0=c0, c1=c1: e.tensor_tensor(
                        q64a[:], p[64:128, :], cosq[:, c0:c1], ALU.mult), tag="a3r4")
                    P.emit("vector", lambda e, p=pq, c0=c0, c1=c1: e.tensor_tensor(
                        q64b[:], p[0:64, :], sinq[:, c0:c1], ALU.mult), tag="a3r5")
                    evr = P.emit("vector", lambda e, qo=qo: e.tensor_tensor(
                        qo[64:128, :], q64a[:], q64b[:], ALU.add),
                        signal=True, tag="a3r6")
                    pe_free[pp] = evr
                    P.emit("sync", lambda e, ot=ot, rg=rg, qo=qo: e.dma_start(
                        out=q_sc[2 * rg, :, ot, :], in_=qo[:, 0:256]),
                        after=[evr], signal=True, tag=f"a3s{ot}_{rg}a")
                    st_ev[pp] = P.emit("sync", lambda e, ot=ot, rg=rg, qo=qo: e.dma_start(
                        out=q_sc[2 * rg + 1, :, ot, :], in_=qo[:, 256:512]),
                        after=[evr], signal=True, tag=f"a3s{ot}_{rg}b")
                use_w[wpar] = P.ev("tensor")
            P.barrier()

        run_phase("a3", _phase_a3)
        def _phase_b():
         with ExitStack() as ph:
            kbuf = sb(ph, "b_k", [128, KV, S], BF16)
            vbuf = sb(ph, "b_v", [128, NT, 1024], BF16)
            mbuf = [sb(ph, "b_m0", [128, 1024], BF16), sb(ph, "b_m1", [128, 1024], BF16)]
            qsl = [sb(ph, "b_q0", [128, NT, 256], BF16), sb(ph, "b_q1", [128, NT, 256], BF16)]
            psb_ = [sb(ph, "b_p0", [128, 1024], BF16), sb(ph, "b_p1", [128, 1024], BF16),
                    sb(ph, "b_p2", [128, 1024], BF16)]
            lacc4 = sb(ph, "b_l4", [128, 1024], F32)
            t512 = sb(ph, "b_t5", [128, 512], F32)
            lacc = sb(ph, "b_l", [128, 256], F32)
            linv = sb(ph, "b_li", [128, 256], F32)
            linr = sb(ph, "b_lr", [1, 256], F32)
            aosl = sb(ph, "b_a", [128, NT, 256], BF16)
            ps_sc = [pt(ph, "b_s0", [128, 1024]), pt(ph, "b_s1", [128, 1024]),
                     pt(ph, "b_s2", [128, 1024])]
            ps_o = [pt(ph, "b_o0", [128, 512]), pt(ph, "b_o1", [128, 512])]

            ek = P.emit("sync", lambda e: e.dma_start(out=kbuf[:], in_=k_sc), signal=True)
            ev0 = P.emit("sync", lambda e: e.dma_start(out=vbuf[:], in_=v_sc), signal=True)
            base = [ek, ev0]

            NPAR = 3
            st = {
                "q_ev": [None, None], "q_use": [None, None],
                "m_ev": [None, None], "m_use": [None, None],
                "ao_st": None, "gpar": 0,
                "exp_done": {i: None for i in range(NPAR)},
                "l_done": {i: None for i in range(NPAR)},
                "av_done": {i: None for i in range(NPAR)},
                "oev": {0: None, 1: None},
                "prev_el": None, "prev_cl": None, "prev_fold": None,
            }
            pend = [None]

            def flush_pend():
                if pend[0] is None:
                    return
                p_ = pend[0]
                pend[0] = None
                s_, h, g, par, eexp = p_["s"], p_["h"], p_["g"], p_["par"], p_["eexp"]
                kv = h // 4
                hpar = h % 2
                spar = p_["spar"]
                po = ps_o[hpar]
                pb = psb_[par]
                psc = ps_sc[par]
                for j in range(4):
                    kt = 4 * g + j
                    eav = P.emit("tensor", lambda e, kv=kv, kt=kt, j=j, o=pb, po=po, g=g, s_=s_: e.matmul(
                        po[:, 0:256], vbuf[:, kt, 128 * kv:128 * (kv + 1)],
                        o[:, 256 * j:256 * (j + 1)],
                        start=(g == 0 and j == 0), stop=(g == s_ and j == 3)),
                        after=([eexp, st["oev"][hpar]] if (g == 0 and j == 0)
                               else ([eexp] if j == 0 else ())),
                        signal=(g == s_ and j == 3), tag=f"bav{s_}_{h}_{g}")
                st["av_done"][par] = eav
                if g != s_:
                    return
                el = P.emit("tensor", lambda e, psc=psc: e.matmul(
                    psc[0:1, 256:512], T["ones_col"][:, 0:1], lacc[:], start=True, stop=True),
                    after=[p_["fold"]], signal=True, tag=f"blr{s_}_{h}")
                st["prev_el"] = el
                cl = P.emit("scalar", lambda e, psc=psc: e.copy(linr[:], psc[0:1, 256:512]),
                            after=[el, st["prev_cl"]], signal=True, tag="blcl")
                st["prev_cl"] = cl
                eb = P.emit("tensor", lambda e, psc=psc: e.matmul(
                    psc[:, 0:256], T["ones_row"][0:1, :], linr[:], start=True, stop=True),
                    after=[cl], signal=True, tag="blbc")
                rec = P.emit("vector", lambda e, psc=psc: e.reciprocal(linv[:], psc[:, 0:256]),
                             after=[eb], signal=True, tag="brec")
                st["exp_done"][par] = rec
                ao_done = P.emit("vector", lambda e, h=h, po=po: e.tensor_tensor(
                    aosl[:, h, :], po[:, 0:256], linv[:], ALU.mult),
                    after=[eav, rec] + ([st["ao_st"]] if h == 0 else []),
                    signal=True, tag=f"bao{s_}_{h}")
                st["oev"][hpar] = ao_done
                if h == H - 1:
                    st["ao_st"] = P.emit("sync", lambda e, s_=s_: e.dma_start(
                        out=ao_sc[s_], in_=aosl[:]),
                        after=[ao_done], signal=True, tag=f"bst{s_}")

            for s_ in range(8):
                mpar = s_ % 2
                spar = s_ % 2
                q = qsl[spar]
                st["m_ev"][mpar] = P.emit("sync", lambda e, s_=s_, m=mbuf[mpar]: e.dma_start(
                    out=m[:], in_=masks_in[s_]),
                    after=[st["m_use"][mpar]] + (base if s_ == 0 else []),
                    signal=True, tag=f"bm{s_}")
                st["q_ev"][spar] = P.emit("sync", lambda e, s_=s_, q=q: e.dma_start(
                    out=q[:], in_=q_sc[s_]),
                    after=[st["q_use"][spar]] + (base if s_ == 0 else []),
                    signal=True, tag=f"bq{s_}")
                for h in range(H):
                    kv = h // 4
                    leng = "vector" if h % 2 == 0 else "gpsimd"
                    for g in range(s_ + 1):
                        par = st["gpar"] % NPAR
                        st["gpar"] += 1
                        psc = ps_sc[par]
                        pb = psb_[par]
                        for j in range(4):
                            kt = 4 * g + j
                            evm = P.emit("tensor", lambda e, kv=kv, kt=kt, j=j, p=psc, h=h, q=q: e.matmul(
                                p[:, 256 * j:256 * (j + 1)],
                                kbuf[:, kv, 128 * kt:128 * (kt + 1)], q[:, h, :],
                                start=True, stop=True),
                                after=([st["q_ev"][spar], st["exp_done"][par]] if j == 0 else ()),
                                signal=(j == 3), tag=f"bsc{s_}_{h}_{g}")
                        if g == s_:
                            mev2 = P.emit("vector", lambda e, p=psc, m=mbuf[mpar]: e.tensor_tensor(
                                p[:], p[:], m[:], ALU.add),
                                after=[evm, st["m_ev"][mpar]], signal=True, tag=f"bmask{s_}_{h}")
                            eafter = [mev2, st["l_done"][par], st["av_done"][par]]
                        else:
                            eafter = [evm, st["l_done"][par], st["av_done"][par]]
                        eexp = P.emit("scalar", lambda e, p=psc, o=pb: e.activation(
                            o[:], p[:], AF.Exp, scale=np.float32(ISQ)),
                            after=eafter, signal=True, tag=f"bexp{s_}_{h}_{g}")
                        st["exp_done"][par] = eexp
                        if g == 0:
                            lev = P.emit("vector", lambda e, o=pb: e.tensor_copy(lacc4[:], o[:]),
                                         after=[eexp, st["prev_fold"], st["prev_el"]],
                                         signal=True, tag="bl0")
                        else:
                            lev = P.emit(leng, lambda e, o=pb: e.tensor_tensor(
                                lacc4[:], lacc4[:], o[:], ALU.add),
                                after=[eexp] + ([st["l_done"][(par + NPAR - 1) % NPAR]] if g == 1 else []),
                                signal=True, tag="bl")
                        st["l_done"][par] = lev
                        flush_pend()
                        pend[0] = {"s": s_, "h": h, "g": g, "par": par, "eexp": eexp,
                                   "fold": None, "spar": spar}
                        if g == s_:
                            P.emit(leng, lambda e: e.tensor_tensor(
                                t512[:], lacc4[:, 0:512], lacc4[:, 512:1024], ALU.add),
                                after=[lev, st["prev_el"]], tag="bf1")
                            fold = P.emit(leng, lambda e: e.tensor_tensor(
                                lacc[:], t512[:, 0:256], t512[:, 256:512], ALU.add),
                                signal=True, tag="bf2")
                            st["prev_fold"] = fold
                            pend[0]["fold"] = fold
                    if h == H - 1:
                        st["q_use"][spar] = P.ev("tensor")
                        st["m_use"][mpar] = P.ev("vector")
                flush_pend()
            P.barrier()

        run_phase("b", _phase_b)
        with ExitStack() as phc:
            bc2i = sb(phc, "bc2i", [128, 2048], F32)
            bc2n = sb(phc, "bc2n", [128, 2048], F32)
            def _phase_c():
             with ExitStack() as ph:
                aores = sb(ph, "c_ao", [128, NT, 2048], BF16)
                wob = [sb(ph, "c_w0", [128, NT, 128], BF16), sb(ph, "c_w1", [128, NT, 128], BF16)]
                ybuf = [sb(ph, "c_y0", [128, 512], F32), sb(ph, "c_y1", [128, 512], F32)]
                sacc = sb(ph, "c_sa", [128, 2048], F32)
                qacc = sb(ph, "c_qa", [128, 2048], F32)
                sqt = sb(ph, "c_sq", [128, 512], F32)
                ps_y = [pt(ph, "c_p0", [128, 512]), pt(ph, "c_p1", [128, 512])]
                ps_st = pt(ph, "c_st", [128, 512])
                ps_bc = pt(ph, "c_bc", [128, 512])

                for s_ in range(8):
                    P.emit("sync", lambda e, s_=s_: e.dma_start(
                        out=aores[:, :, 256 * s_:256 * (s_ + 1)], in_=ao_sc[s_]),
                        signal=True, tag=f"cao{s_}")
                eao = P.ev("sync")
                w_ev = [None, None]
                w_use = [None, None]
                evac = {0: None, 1: None}
                y_st = {0: None, 1: None}
                stat_last = None
                for ht in range(NT):
                    wpar = ht % 2
                    wb = wob[wpar]
                    w_ev[wpar] = P.emit("gpsimd", lambda e, ht=ht, w=wb: e.dma_start(
                        out=w[:], in_=wo_cm[ht]), after=[w_use[wpar]], signal=True, tag=f"cw{ht}")
                    for rg in range(4):
                        pp = rg % 2
                        py = ps_y[pp]
                        yb = ybuf[pp]
                        for t in range(NT):
                            evm = P.emit("tensor", lambda e, w=wb, t=t, rg=rg, p=py: e.matmul(
                                p[:], w[:, t, :], aores[:, t, 512 * rg:512 * (rg + 1)],
                                start=(t == 0), stop=(t == NT - 1)),
                                after=([w_ev[wpar], eao, evac[pp]] if t == 0 else ()),
                                signal=(t == NT - 1), tag=f"cmm{ht}_{rg}")
                        ec = P.emit("vector", lambda e, yb=yb, p=py: e.tensor_copy(yb[:], p[:]),
                                    after=[evm, y_st[pp]], signal=True, tag=f"ccp{ht}_{rg}")
                        evac[pp] = ec
                        y_st[pp] = P.emit("gpsimd", lambda e, ht=ht, rg=rg, yb=yb: e.dma_start(
                            out=y_sc[rg, :, ht, :], in_=yb[:]), after=[ec], signal=True,
                            tag=f"cst{ht}_{rg}")
                        if ht == 0:
                            P.emit("vector", lambda e, yb=yb, rg=rg: e.tensor_copy(
                                sacc[:, 512 * rg:512 * (rg + 1)], yb[:]), tag="cs0")
                            P.emit("vector", lambda e, yb=yb: e.tensor_tensor(
                                sqt[:], yb[:], yb[:], ALU.mult), tag="cm0")
                            stat_last = P.emit("vector", lambda e, rg=rg: e.tensor_copy(
                                qacc[:, 512 * rg:512 * (rg + 1)], sqt[:]),
                                signal=(rg == 3), tag="cq0")
                        else:
                            P.emit("vector", lambda e, yb=yb, rg=rg: e.tensor_tensor(
                                sacc[:, 512 * rg:512 * (rg + 1)],
                                sacc[:, 512 * rg:512 * (rg + 1)], yb[:], ALU.add), tag="cs")
                            P.emit("vector", lambda e, yb=yb: e.tensor_tensor(
                                sqt[:], yb[:], yb[:], ALU.mult), tag="cm")
                            stat_last = P.emit("vector", lambda e, rg=rg: e.tensor_tensor(
                                qacc[:, 512 * rg:512 * (rg + 1)],
                                qacc[:, 512 * rg:512 * (rg + 1)], sqt[:], ALU.add),
                                signal=(ht == NT - 1), tag="cq")
                    w_use[wpar] = P.ev("tensor")
                for c8 in range(8):
                    ln_stats(P, T, sacc[:, 256 * c8:256 * (c8 + 1)],
                             qacc[:, 256 * c8:256 * (c8 + 1)], ps_st, ps_bc, D,
                             bc2i[:, 256 * c8:256 * (c8 + 1)],
                             bc2n[:, 256 * c8:256 * (c8 + 1)], [stat_last])
                P.barrier()

            run_phase("c", _phase_c)
            with ExitStack() as ph:
                bc3i = sb(ph, "bc3i", [128, 2048], F32)
                bc3n = sb(ph, "bc3n", [128, 2048], F32)
                def _phase_d():
                 with ExitStack() as ph2:
                    ybig = sb(ph2, "d_y", [128, NT, 512], F32)
                    xbig = sb(ph2, "d_x", [128, NT, 512], F32)
                    zt = [sb(ph2, "d_z0", [128, 512], F32), sb(ph2, "d_z1", [128, 512], F32)]
                    tmp = sb(ph2, "d_t", [128, 512], F32)
                    sacc2 = sb(ph2, "d_sa", [128, 2048], F32)
                    qacc2 = sb(ph2, "d_qa", [128, 2048], F32)
                    ps_st = pt(ph2, "d_st", [128, 512])
                    ps_bc = pt(ph2, "d_bc", [128, 512])

                    y_ev = None
                    z_st = {0: None, 1: None}
                    stat_last = None
                    for cg in range(4):
                        vprev = P.ev("vector")
                        y_ev = P.emit("sync", lambda e, cg=cg: e.dma_start(
                            out=ybig[:], in_=y_sc[cg]), after=[vprev], signal=True, tag=f"dy{cg}")
                        P.emit("sync", lambda e, cg=cg: e.dma_start(
                            out=xbig[:, :, 0:256], in_=x_cm[CHUNKS[2 * cg]]),
                            after=[vprev], signal=True, tag=f"dxa{cg}")
                        x_ev = P.emit("sync", lambda e, cg=cg: e.dma_start(
                            out=xbig[:, :, 256:512], in_=x_cm[CHUNKS[2 * cg + 1]]),
                            signal=True, tag=f"dxb{cg}")
                        for t in range(NT):
                            tp = t % 2
                            z = zt[tp]
                            cols = slice(512 * cg, 512 * (cg + 1))
                            P.emit("vector", lambda e, t=t, cols=cols: e.tensor_tensor(
                                tmp[:], ybig[:, t, :], bc2i[:, cols], ALU.mult),
                                after=([y_ev] if t == 0 else ()), tag="d1")
                            P.emit("vector", lambda e, cols=cols: e.tensor_tensor(
                                tmp[:], tmp[:], bc2n[:, cols], ALU.subtract), tag="d2")
                            ez = P.emit("vector", lambda e, t=t, z=z: e.tensor_tensor(
                                z[:], xbig[:, t, :], tmp[:], ALU.add),
                                after=[x_ev if t == 0 else None, z_st[tp]],
                                signal=True, tag="d3")
                            z_st[tp] = P.emit("gpsimd", lambda e, cg=cg, t=t, z=z: e.dma_start(
                                out=z_sc[cg, :, t, :], in_=z[:]), after=[ez], signal=True,
                                tag=f"dz{cg}_{t}")
                            if cg == 0 and t == 0:
                                P.emit("vector", lambda e, z=z, cols=cols: e.tensor_copy(
                                    sacc2[:, cols], z[:]), tag="ds0")
                                P.emit("vector", lambda e, z=z: e.tensor_tensor(
                                    tmp[:], z[:], z[:], ALU.mult), tag="dm0")
                                stat_last = P.emit("vector", lambda e, cols=cols: e.tensor_copy(
                                    qacc2[:, cols], tmp[:]), tag="dq0")
                            elif t == 0:
                                P.emit("vector", lambda e, z=z, cols=cols: e.tensor_copy(
                                    sacc2[:, cols], z[:]), tag="ds0b")
                                P.emit("vector", lambda e, z=z: e.tensor_tensor(
                                    tmp[:], z[:], z[:], ALU.mult), tag="dm0b")
                                stat_last = P.emit("vector", lambda e, cols=cols: e.tensor_copy(
                                    qacc2[:, cols], tmp[:]), tag="dq0b")
                            else:
                                P.emit("vector", lambda e, z=z, cols=cols: e.tensor_tensor(
                                    sacc2[:, cols], sacc2[:, cols], z[:], ALU.add), tag="ds")
                                P.emit("vector", lambda e, z=z: e.tensor_tensor(
                                    tmp[:], z[:], z[:], ALU.mult), tag="dm")
                                stat_last = P.emit("vector", lambda e, cols=cols: e.tensor_tensor(
                                    qacc2[:, cols], qacc2[:, cols], tmp[:], ALU.add),
                                    signal=(t == NT - 1), tag="dq")
                    for c8 in range(8):
                        ln_stats(P, T, sacc2[:, 256 * c8:256 * (c8 + 1)],
                                 qacc2[:, 256 * c8:256 * (c8 + 1)], ps_st, ps_bc, D,
                                 bc3i[:, 256 * c8:256 * (c8 + 1)],
                                 bc3n[:, 256 * c8:256 * (c8 + 1)], [stat_last])
                    P.barrier()

                run_phase("d", _phase_d)
                def _phase_d2():
                 with ExitStack() as ph2:
                    zb = sb(ph2, "d2_z", [128, NT, 256], F32)
                    x2b = [sb(ph2, "d2_x0", [128, NT, 256], BF16),
                           sb(ph2, "d2_x1", [128, NT, 256], BF16)]
                    tmp = sb(ph2, "d2_t", [128, 256], F32)
                    x2st = [None, None]
                    for cc2 in range(8):
                        cg, hf = divmod(cc2, 2)
                        par = cc2 % 2
                        xo = x2b[par]
                        vprev = P.ev("vector")
                        zev = None
                        for t in range(NT):
                            zev = P.emit("sync", lambda e, cg=cg, hf=hf, t=t: e.dma_start(
                                out=zb[:, t, :], in_=z_sc[cg, :, t, 256 * hf:256 * (hf + 1)]),
                                after=([vprev] if t == 0 else ()), signal=True, tag=f"d2z{cc2}_{t}")
                        cols = slice(256 * cc2, 256 * (cc2 + 1))
                        aev = None
                        for t in range(NT):
                            P.emit("vector", lambda e, t=t, cols=cols: e.tensor_tensor(
                                tmp[:], zb[:, t, :], bc3i[:, cols], ALU.mult),
                                after=([zev, x2st[par]] if t == 0 else ()), tag="d21")
                            aev = P.emit("vector", lambda e, t=t, cols=cols, xo=xo: e.tensor_tensor(
                                xo[:, t, :], tmp[:], bc3n[:, cols], ALU.subtract),
                                signal=(t == NT - 1), tag="d22")
                        x2st[par] = P.emit("sync", lambda e, cc2=cc2, xo=xo: e.dma_start(
                            out=x2_sc[cc2], in_=xo[:]), after=[aev], signal=True, tag=f"d2s{cc2}")
                    P.barrier()

                run_phase("d2", _phase_d2)
        def _phase_e():
         with ExitStack() as ph:
            xres = sb(ph, "e_x", [128, NT, 2048], BF16)
            qgb = [sb(ph, "e_qg0", [128, D], I8), sb(ph, "e_qg1", [128, D], I8)]
            qub = [sb(ph, "e_qu0", [128, D], I8), sb(ph, "e_qu1", [128, D], I8)]
            deqb_g = sb(ph, "e_dqg", [128, D], BF16)
            deqb_u = sb(ph, "e_dqu", [128, D], BF16)
            lhg = [sb(ph, "e_lg0", [128, D], BF16), sb(ph, "e_lg1", [128, D], BF16)]
            lhu0 = sb(ph, "e_lu0", [128, D], BF16)
            lhu = [lhu0, lhu0]
            scb = [sb(ph, "e_sc0", [128, 4, 32], F32), sb(ph, "e_sc1", [128, 4, 32], F32)]
            zsb = [sb(ph, "e_zs0", [128, 2, 32], F32), sb(ph, "e_zs1", [128, 2, 32], F32)]
            sgt = sb(ph, "e_sg", [128, 512], BF16)
            hbuf = sb(ph, "e_h", [128, 2048], BF16)
            ps_g = [pt(ph, "e_g0", [128, 512]), pt(ph, "e_g1", [128, 512])]
            ps_u = [pt(ph, "e_u0", [128, 512]), pt(ph, "e_u1", [128, 512])]
            ps_t = [pt(ph, "e_t0", [128, 1024], BF16), pt(ph, "e_t1", [128, 1024], BF16)]

            for s_ in range(8):
                P.emit("sync", lambda e, s_=s_: e.dma_start(
                    out=xres[:, :, 256 * s_:256 * (s_ + 1)], in_=x2_sc[s_]),
                    signal=True, tag=f"ex{s_}")
            xall = P.ev("sync")

            deq_g_hist = {0: None, 1: None}
            deq_u_hist = {0: None, 1: None}
            mm_hist = {0: None, 1: None}
            mm_prev = None
            trg_pe_prev = None
            tru_pe_prev = None
            tr_free = {0: None, 1: None}
            mm_free = {"g": {0: None, 1: None}, "u": {0: None, 1: None}}
            sil_done = {0: None, 1: None}
            h_st = {}
            for isl in range(NISL):
                par = isl % 2
                scp = scb[par]
                zsp = zsb[par]
                qg_ev = P.emit("sync", lambda e, isl=isl, b=qgb[par]: e.dma_start(
                    out=b[:], in_=qwg_cm[isl]), after=[deq_g_hist[par]],
                    signal=True, tag=f"eqg{isl}")
                qu_ev = P.emit("sync", lambda e, isl=isl, b=qub[par]: e.dma_start(
                    out=b[:], in_=qwu_cm[isl]), after=[deq_u_hist[par]],
                    signal=True, tag=f"equ{isl}")
                sc_ev = P.emit("sync", lambda e, isl=isl, s=scp: e.dma_start(
                    out=s[:], in_=gu_sc[isl]),
                    after=[deq_g_hist[par], deq_u_hist[par]], signal=True, tag=f"esc{isl}")
                ezs = P.emit("vector", lambda e, s=scp, z=zsp: e.scalar_tensor_tensor(
                    z[:, 0, :], s[:, 1, :], -1.0, s[:, 0, :], ALU.mult, ALU.mult),
                    after=[sc_ev, deq_g_hist[par]], signal=True, tag="ezs_g")
                ezs2 = P.emit("vector", lambda e, s=scp, z=zsp: e.scalar_tensor_tensor(
                    z[:, 1, :], s[:, 3, :], -1.0, s[:, 2, :], ALU.mult, ALU.mult),
                    after=[sc_ev, deq_u_hist[par]], signal=True, tag="ezs_u")
                dq = None
                for gi in range(NT):
                    dq = P.emit("scalar", lambda e, gi=gi, s=scp, z=zsp, b=qgb[par]: e.activation(
                        deqb_g[:, 128 * gi:128 * (gi + 1)], b[:, 128 * gi:128 * (gi + 1)],
                        AF.Identity, scale=s[:, 0, gi:gi + 1], bias=z[:, 0, gi:gi + 1]),
                        after=([qg_ev, ezs, trg_pe_prev] if gi == 0 else ()),
                        signal=(gi % 8 == 7), tag=f"edqg{isl}_{gi}")
                deq_g_done = dq
                deq_g_hist[par] = dq
                for bb in range(4):
                    tpar = bb % 2
                    ptr = ps_t[tpar]
                    for j in range(8):
                        t = 8 * bb + j
                        etr = P.emit("tensor", lambda e, t=t, j=j, p=ptr: e.transpose(
                            p[:, 128 * j:128 * (j + 1)], deqb_g[:, 128 * t:128 * (t + 1)],
                            T["ident"][:]),
                            after=([deq_g_done, tr_free[tpar], eid] if j == 0 else ()),
                            signal=(j == 7), tag=f"etrg{isl}_{bb}")
                    tr_free[tpar] = P.emit("scalar", lambda e, bb=bb, p=ptr, o=lhg[par]: e.copy(
                        o[:, 1024 * bb:1024 * (bb + 1)], p[:]),
                        after=[etr, mm_hist[par]], signal=True, tag=f"etge{isl}_{bb}")
                trg_pe_prev = etr
                ev_lhg = [tr_free[0], tr_free[1]]
                dq = None
                for gi in range(NT):
                    dq = P.emit("vector", lambda e, gi=gi, s=scp, z=zsp, b=qub[par]: e.tensor_scalar(
                        deqb_u[:, 128 * gi:128 * (gi + 1)], b[:, 128 * gi:128 * (gi + 1)],
                        s[:, 2, gi:gi + 1], z[:, 1, gi:gi + 1], ALU.mult, ALU.add),
                        after=([qu_ev, ezs2, tru_pe_prev] if gi == 0 else ()),
                        signal=(gi % 8 == 7), tag=f"edqu{isl}_{gi}")
                deq_u_done = dq
                deq_u_hist[par] = dq
                for bb in range(4):
                    tpar = bb % 2
                    ptr = ps_t[tpar]
                    for j in range(8):
                        t = 8 * bb + j
                        etr = P.emit("tensor", lambda e, t=t, j=j, p=ptr: e.transpose(
                            p[:, 128 * j:128 * (j + 1)], deqb_u[:, 128 * t:128 * (t + 1)],
                            T["ident"][:]),
                            after=([deq_u_done, tr_free[tpar]] if j == 0 else ()),
                            signal=(j == 7), tag=f"etru{isl}_{bb}")
                    tr_free[tpar] = P.emit("vector", lambda e, bb=bb, p=ptr, o=lhu[par]: e.tensor_copy(
                        o[:, 1024 * bb:1024 * (bb + 1)], p[:]),
                        after=[etr, mm_prev], signal=True, tag=f"etue{isl}_{bb}")
                tru_pe_prev = etr
                ev_lhu = [tr_free[0], tr_free[1]]
                for rg in range(4):
                    pp = rg % 2
                    pg, pu = ps_g[pp], ps_u[pp]
                    for t in range(NT):
                        eg = P.emit("tensor", lambda e, t=t, rg=rg, p=pg, w=lhg[par]: e.matmul(
                            p[:], w[:, 128 * t:128 * (t + 1)],
                            xres[:, t, 512 * rg:512 * (rg + 1)],
                            start=(t == 0), stop=(t == NT - 1)),
                            after=(ev_lhg + [xall, mm_free["g"][pp]] if t == 0 else ()),
                            signal=(t == NT - 1), tag=f"emg{isl}_{rg}")
                    for t in range(NT):
                        eu = P.emit("tensor", lambda e, t=t, rg=rg, p=pu, w=lhu[par]: e.matmul(
                            p[:], w[:, 128 * t:128 * (t + 1)],
                            xres[:, t, 512 * rg:512 * (rg + 1)],
                            start=(t == 0), stop=(t == NT - 1)),
                            after=(ev_lhu + [mm_free["u"][pp]] if t == 0 else ()),
                            signal=(t == NT - 1), tag=f"emu{isl}_{rg}")
                    esil = P.emit("scalar", lambda e, p=pg: e.activation(
                        sgt[:], p[:], AF.Silu), after=[eg, sil_done[pp]], signal=True,
                        tag=f"esil{isl}_{rg}")
                    mm_free["g"][pp] = esil
                    eh = P.emit("vector", lambda e, rg=rg, p=pu: e.tensor_tensor(
                        hbuf[:, 512 * rg:512 * (rg + 1)], p[:], sgt[:], ALU.mult),
                        after=[eu, esil, h_st.get(rg)], signal=True, tag=f"eh{isl}_{rg}")
                    mm_free["u"][pp] = eh
                    sil_done[pp] = eh
                    h_st[rg] = P.emit("gpsimd", lambda e, isl=isl, rg=rg: e.dma_start(
                        out=h_sc[rg, :, isl, :], in_=hbuf[:, 512 * rg:512 * (rg + 1)]),
                        after=[eh], signal=True, tag=f"ehst{isl}_{rg}")
                mm_hist[par] = P.ev("tensor")
                mm_prev = mm_hist[par]
            P.barrier()

        run_phase("e", _phase_e)
        def _phase_f0():
         with ExitStack() as ph:
            qdb = [sb(ph, "f0_q0", [128, II], I8), sb(ph, "f0_q1", [128, II], I8)]
            deqd = sb(ph, "f0_dq", [128, II], BF16)
            stg = [sb(ph, "f0_s0", [128, II], BF16), sb(ph, "f0_s1", [128, II], BF16)]
            dscb = [sb(ph, "f0_c0", [128, 2, NISL], F32), sb(ph, "f0_c1", [128, 2, NISL], F32)]
            zsd = sb(ph, "f0_zs", [128, NISL], F32)
            ps_t = [pt(ph, "f0_t0", [128, 1024], BF16), pt(ph, "f0_t1", [128, 1024], BF16)]

            q_ev = [None, None]
            qfree = [[], []]
            deq_done = None
            tr_free = {0: None, 1: None}
            st_ev = [None, None]
            zs_used = []
            for ot in range(NT):
                par = ot % 2
                qd = qdb[par]
                dc = dscb[par]
                q_ev[par] = P.emit("sync", lambda e, ot=ot, q=qd: e.dma_start(
                    out=q[:], in_=qwd_cm[ot]), after=qfree[par], signal=True, tag=f"f0q{ot}")
                sc_ev = P.emit("sync", lambda e, ot=ot, d=dc: e.dma_start(
                    out=d[:], in_=d_sc[ot]), after=qfree[par], signal=True, tag=f"f0c{ot}")
                ezs = P.emit("vector", lambda e, d=dc: e.scalar_tensor_tensor(
                    zsd[:], d[:, 1, :], -1.0, d[:, 0, :], ALU.mult, ALU.mult),
                    after=[sc_ev] + zs_used, signal=True, tag="f0zs")
                dq = None
                for gi in range(NISL):
                    eng = "scalar" if (gi % 2 == 0) else "vector"
                    if eng == "scalar":
                        dq = P.emit("scalar", lambda e, gi=gi, q=qd, d=dc: e.activation(
                            deqd[:, 128 * gi:128 * (gi + 1)], q[:, 128 * gi:128 * (gi + 1)],
                            AF.Identity, scale=d[:, 0, gi:gi + 1], bias=zsd[:, gi:gi + 1]),
                            after=([q_ev[par], ezs, deq_done] if gi <= 1 else ()),
                            signal=(gi == NISL - 2), tag=f"f0dq{ot}_{gi}")
                    else:
                        dq2 = P.emit("vector", lambda e, gi=gi, q=qd, d=dc: e.tensor_scalar(
                            deqd[:, 128 * gi:128 * (gi + 1)], q[:, 128 * gi:128 * (gi + 1)],
                            d[:, 0, gi:gi + 1], zsd[:, gi:gi + 1], ALU.mult, ALU.add),
                            after=([q_ev[par], ezs, deq_done] if gi <= 1 else ()),
                            signal=(gi == NISL - 1), tag=f"f0dqv{ot}_{gi}")
                deq_s, deq_v = dq, dq2
                zs_used = [deq_s, deq_v]
                qfree[par] = [deq_s, deq_v]
                sg = stg[par]
                nbb = (NISL + 7) // 8
                for bb in range(nbb):
                    tpar = bb % 2
                    ptr = ps_t[tpar]
                    n_in = min(8, NISL - 8 * bb)
                    for j in range(n_in):
                        t = 8 * bb + j
                        etr = P.emit("tensor", lambda e, t=t, j=j, p=ptr: e.transpose(
                            p[:, 128 * j:128 * (j + 1)], deqd[:, 128 * t:128 * (t + 1)],
                            T["ident"][:]),
                            after=([deq_s, deq_v, tr_free[tpar], eid] if j == 0 else ()),
                            signal=(j == n_in - 1), tag=f"f0tr{ot}_{bb}")
                    eng = "scalar" if (bb % 2 == 0) else "vector"
                    tr_free[tpar] = P.emit(eng, lambda e, bb=bb, n_in=n_in, p=ptr, sg=sg:
                        (e.copy if hasattr(e, 'copy') else e.tensor_copy)(
                        sg[:, 1024 * bb:1024 * bb + 128 * n_in], p[:, 0:128 * n_in]),
                        after=[etr, st_ev[par]], signal=True, tag=f"f0ev{ot}_{bb}")
                deq_done = tr_free[(nbb - 1) % 2]
                st_ev[par] = P.emit("gpsimd", lambda e, ot=ot, sg=sg: e.dma_start(
                    out=wd_sc[ot], in_=sg[:].rearrange("p (a b) -> p a b", a=NISL)),
                    after=[tr_free[0], tr_free[1]], signal=True, tag=f"f0st{ot}")
            P.barrier()

        run_phase("f0", _phase_f0)
        def _phase_f():
         with ExitStack() as ph:
            bc4i = sb(ph, "f_bci", [128, 512], F32)
            bc4n = sb(ph, "f_bcn", [128, 512], F32)
            hres = sb(ph, "f_h", [128, NISL, 512], BF16)
            wdb = [sb(ph, "f_w0", [128, NISL, 128], BF16), sb(ph, "f_w1", [128, NISL, 128], BF16)]
            o2b = [sb(ph, "f_o0", [128, 512], F32), sb(ph, "f_o1", [128, 512], F32)]
            sacc = sb(ph, "f_sa", [128, 512], F32)
            qacc = sb(ph, "f_qa", [128, 512], F32)
            sqt = sb(ph, "f_sq", [128, 512], F32)
            ztl = [sb(ph, "f_z0", [128, 512], F32), sb(ph, "f_z1", [128, 512], F32)]
            otl = [sb(ph, "f_t0", [128, 512], F32), sb(ph, "f_t1", [128, 512], F32)]
            oub = [sb(ph, "f_u0", [128, 512], F32), sb(ph, "f_u1", [128, 512], F32)]
            ps_f = [pt(ph, "f_p0", [128, 512]), pt(ph, "f_p1", [128, 512])]
            ps_st = pt(ph, "f_st", [128, 512])
            ps_bc = pt(ph, "f_bc", [128, 512])

            h_ev = None
            h_use = None
            w_ev = [None, None]
            w_use = [None, None]
            evac = {0: None, 1: None}
            o_st = {0: None, 1: None}
            ap_done = None
            out_st = {0: None, 1: None}
            zld = {0: None, 1: None}
            old = {0: None, 1: None}
            for rc in range(4):
                h_ev_a = P.emit("sync", lambda e, rc=rc: e.dma_start(
                    out=hres[:, 0:43, :], in_=h_sc[rc, :, 0:43, :]),
                    after=[h_use], signal=True, tag=f"fh{rc}a")
                h_ev_b = P.emit("sync", lambda e, rc=rc: e.dma_start(
                    out=hres[:, 43:NISL, :], in_=h_sc[rc, :, 43:NISL, :]),
                    signal=True, tag=f"fh{rc}b")
                h_ev = [h_ev_a, h_ev_b]
                stat_last = None
                for ot in range(NT):
                    wpar = ot % 2
                    wb = wdb[wpar]
                    w_ev[wpar] = P.emit("sync", lambda e, ot=ot, w=wb: e.dma_start(
                        out=w[:], in_=wd_sc[ot]), after=[w_use[wpar]], signal=True, tag=f"fw{rc}_{ot}")
                    pp = ot % 2
                    pf = ps_f[pp]
                    ob = o2b[pp]
                    for it in range(NISL):
                        evm = P.emit("tensor", lambda e, it=it, w=wb, p=pf: e.matmul(
                            p[:], w[:, it, :], hres[:, it, :],
                            start=(it == 0), stop=(it == NISL - 1)),
                            after=(h_ev + [w_ev[wpar], evac[pp]] if it == 0 else ()),
                            signal=(it == NISL - 1), tag=f"fmm{rc}_{ot}")
                    ec = P.emit("vector", lambda e, ob=ob, p=pf: e.tensor_copy(ob[:], p[:]),
                                after=[evm, o_st[pp]], signal=True, tag=f"fcp{rc}_{ot}")
                    evac[pp] = ec
                    o_st[pp] = P.emit("gpsimd", lambda e, rc=rc, ot=ot, ob=ob: e.dma_start(
                        out=o2_sc[rc, :, ot, :], in_=ob[:]), after=[ec], signal=True,
                        tag=f"fst{rc}_{ot}")
                    if ot == 0:
                        P.emit("vector", lambda e, ob=ob: e.tensor_copy(sacc[:], ob[:]),
                               after=[_LN_STATE["sum_mm"]], tag="fs0")
                        P.emit("vector", lambda e, ob=ob: e.tensor_tensor(
                            sqt[:], ob[:], ob[:], ALU.mult), tag="fm0")
                        stat_last = P.emit("vector", lambda e: e.tensor_copy(qacc[:], sqt[:]), tag="fq0")
                    else:
                        P.emit("vector", lambda e, ob=ob: e.tensor_tensor(
                            sacc[:], sacc[:], ob[:], ALU.add), tag="fs")
                        P.emit("vector", lambda e, ob=ob: e.tensor_tensor(
                            sqt[:], ob[:], ob[:], ALU.mult), tag="fm")
                        stat_last = P.emit("vector", lambda e: e.tensor_tensor(
                            qacc[:], qacc[:], sqt[:], ALU.add),
                            signal=(ot == NT - 1), tag="fq")
                    w_use[wpar] = P.ev("tensor")
                h_use = P.ev("tensor")
                for c2 in range(2):
                    ln_stats(P, T, sacc[:, 256 * c2:256 * (c2 + 1)],
                             qacc[:, 256 * c2:256 * (c2 + 1)], ps_st, ps_bc, D,
                             bc4i[:, 256 * c2:256 * (c2 + 1)],
                             bc4n[:, 256 * c2:256 * (c2 + 1)], [stat_last])
                bc_ev = [P.ev("scalar")]
                for t in range(NT):
                    tp = t % 2
                    z = ztl[tp]
                    o2t = otl[tp]
                    ou = oub[tp]
                    ez = P.emit("sync", lambda e, rc=rc, t=t, z=z: e.dma_start(
                        out=z[:], in_=z_sc[rc, :, t, :]), after=[ap_done], signal=True,
                        tag=f"fz{rc}_{t}")
                    eo = P.emit("sync", lambda e, rc=rc, t=t, o=o2t: e.dma_start(
                        out=o[:], in_=o2_sc[rc, :, t, :]),
                        after=([o_st[0], o_st[1], ap_done] if t == 0 else [ap_done]),
                        signal=True, tag=f"fo{rc}_{t}")
                    P.emit("vector", lambda e, o=o2t: e.tensor_tensor(
                        sqt[:], o[:], bc4i[:], ALU.mult),
                        after=[eo] + bc_ev + [out_st[tp]], tag="fa1")
                    P.emit("vector", lambda e: e.tensor_tensor(
                        sqt[:], sqt[:], bc4n[:], ALU.subtract), tag="fa2")
                    ap_done = P.emit("vector", lambda e, z=z, ou=ou: e.tensor_tensor(
                        ou[:], z[:], sqt[:], ALU.add), after=[ez], signal=True, tag="fa3")
                    out_st[tp] = P.emit("gpsimd", lambda e, rc=rc, t=t, ou=ou: e.dma_start(
                        out=out_cm[rc, :, t, :], in_=ou[:]), after=[ap_done], signal=True,
                        tag=f"fout{rc}_{t}")
            P.barrier()

        run_phase("f", _phase_f)
        with nc.Block() as block:
            P.build(block)
    return nc



_NC_CACHE = {}


def _get_nc(taps=()):
    key = tuple(sorted(taps))
    if key not in _NC_CACHE:
        _NC_CACHE[key] = build_nc(taps=key)
    return _NC_CACHE[key]


def _rope_tables():
    pos = np.arange(S, dtype=np.float32)[:, None]
    freqs = np.exp(-np.arange(0, HD, 2, dtype=np.float32) * (math.log(10000.0) / HD))
    ang = pos * freqs
    return np.ascontiguousarray(np.cos(ang).T), np.ascontiguousarray(np.sin(ang).T)


def make_in_maps(inputs):
    x = np.asarray(inputs["L0_attn_hidden_states"], dtype=np.float32)
    Wq = np.asarray(inputs["L0_attn_Wq"], dtype=np.float32)
    Wk = np.asarray(inputs["L0_attn_Wk"], dtype=np.float32)
    Wv = np.asarray(inputs["L0_attn_Wv"], dtype=np.float32)
    Wo = np.asarray(inputs["L0_attn_Wo"], dtype=np.float32)
    qwg = np.asarray(inputs["L0_ffn_qw_gate"], dtype=np.int32)
    qwu = np.asarray(inputs["L0_ffn_qw_up"], dtype=np.int32)
    qwd = np.asarray(inputs["L0_ffn_qw_down"], dtype=np.int32)
    sg = np.asarray(inputs["L0_ffn_s_gate"], dtype=np.float32)
    zg = np.asarray(inputs["L0_ffn_z_gate"], dtype=np.float32)
    su = np.asarray(inputs["L0_ffn_s_up"], dtype=np.float32)
    zu = np.asarray(inputs["L0_ffn_z_up"], dtype=np.float32)
    sd = np.asarray(inputs["L0_ffn_s_down"], dtype=np.float32)
    zd = np.asarray(inputs["L0_ffn_z_down"], dtype=np.float32)

    wk_cm = np.ascontiguousarray(Wk.T.reshape(NT, 128, 1024).transpose(1, 0, 2))
    wv_cm = np.ascontiguousarray(Wv.T.reshape(NT, 128, 1024).transpose(1, 0, 2))
    wq_cm = np.ascontiguousarray(Wq.reshape(NT, 128, NT, 128).transpose(0, 3, 2, 1))
    wo_cm = np.ascontiguousarray(Wo.reshape(NT, 128, NT, 128).transpose(0, 3, 2, 1))
    qwg_cm = np.ascontiguousarray(qwg.reshape(NISL, 128, D).astype(np.int8))
    qwu_cm = np.ascontiguousarray(qwu.reshape(NISL, 128, D).astype(np.int8))
    qwd_cm = np.ascontiguousarray(qwd.reshape(NT, 128, II).astype(np.int8))
    gu_scm = np.ascontiguousarray(np.stack(
        [sg.reshape(NISL, 128, 32), zg.reshape(NISL, 128, 32),
         su.reshape(NISL, 128, 32), zu.reshape(NISL, 128, 32)], axis=2))
    d_scm = np.ascontiguousarray(np.stack(
        [sd.reshape(NT, 128, NISL), zd.reshape(NT, 128, NISL)], axis=2))
    cosT, sinT = _rope_tables()
    rope_k = np.ascontiguousarray(np.stack([cosT, sinT], axis=0))
    ident = np.eye(128, dtype=ml_dtypes.bfloat16)

    in_maps = []
    for core in range(8):
        b = core // 2
        chunks = EVEN_CHUNKS if core % 2 == 0 else ODD_CHUNKS
        own_cols = np.concatenate([np.arange(256 * c, 256 * (c + 1)) for c in chunks])
        truepos = np.arange(NCC)
        if core % 2 == 1:
            for s_ in range(8):
                truepos[EVEN_CHUNKS[s_]] = ODD_CHUNKS[s_]
                truepos[ODD_CHUNKS[s_]] = EVEN_CHUNKS[s_]
        x_cm = np.ascontiguousarray(
            x[b].reshape(NCC, 256, NT, 128).transpose(0, 3, 2, 1)[truepos])
        rope_q = np.ascontiguousarray(
            np.stack([cosT[:, own_cols], sinT[:, own_cols]], axis=0))
        ck = cosT.reshape(64, NCC, 256)[:, truepos, :].reshape(64, S)
        sk = sinT.reshape(64, NCC, 256)[:, truepos, :].reshape(64, S)
        rope_k = np.ascontiguousarray(np.stack([ck, sk], axis=0))
        masks = np.zeros((8, 128, 1024), dtype=np.float32)
        for s_ in range(8):
            c = chunks[s_]
            n_kt = 4 * (s_ + 1)
            for j in range(4):
                kt = n_kt - 4 + j
                unit = kt // 2
                true_k = 256 * truepos[unit] + (kt % 2) * 128 + np.arange(128)[:, None]
                gq = 256 * c + np.arange(256)[None, :]
                masks[s_, :, 256 * j:256 * (j + 1)] = np.where(true_k <= gq, 0.0, -1e9)
        in_maps.append({
            "x_cm": x_cm, "wk_cm": wk_cm, "wv_cm": wv_cm, "wq_cm": wq_cm,
            "wo_cm": wo_cm, "qwg_cm": qwg_cm, "qwu_cm": qwu_cm, "qwd_cm": qwd_cm,
            "gu_sc": gu_scm, "d_sc": d_scm, "rope_q": rope_q, "rope_k": rope_k,
            "masks_in": masks.astype(ml_dtypes.bfloat16), "ident_in": ident,
        })
    return in_maps


def assemble_output(results):
    out = np.zeros((B, S, D), dtype=np.float32)
    for core in range(8):
        b = core // 2
        chunks = EVEN_CHUNKS if core % 2 == 0 else ODD_CHUNKS
        own_cols = np.concatenate([np.arange(256 * c, 256 * (c + 1)) for c in chunks])
        oc = np.asarray(results[core]["out_cm"], dtype=np.float32)
        outT = oc.transpose(2, 1, 0, 3).reshape(D, 2048)
        out[b, own_cols, :] = outT.T
    return out


def kernel(**inputs):
    nc = _get_nc()
    in_maps = make_in_maps(inputs)
    res = run_bass_kernel_spmd(nc, in_maps, core_ids=list(range(8)))
    return assemble_output(res.results)
